# version 12
# baseline (speedup 1.0000x reference)
import numpy as np
import ml_dtypes
from contextlib import ExitStack

import concourse.bass as bass
import concourse.mybir as mybir
from concourse.bass_utils import run_bass_kernel_spmd

F32 = mybir.dt.float32
BF16 = mybir.dt.bfloat16
I32 = mybir.dt.int32
AO = mybir.AluOpType
AF = mybir.ActivationFunctionType
AXX = mybir.AxisListType.X

NCORE = 8
B, S, D = 4, 2048, 2048
T = 1024
NDC = D // 128
CH = 128
NCH = T // CH
TW = 512
I_RAW = 6464
I_PAD = 6528
NIC = I_PAD // 128
FFN_CNT = float(I_RAW * D)
BF = ml_dtypes.bfloat16



def _A(x):
    if x is None or isinstance(x, (int, float)):
        return x
    if hasattr(x, "tensor"):
        return x
    try:
        return x[:]
    except Exception:
        return x

class Sched:
    STREAMS = ("sync", "scalar", "tensor", "vector", "gpsimd")

    def __init__(self, nc):
        self.nc = nc
        self.prog = {s: [] for s in self.STREAMS}
        self.semh = {}
        self.cnt = {}
        for name in ("lds", "lda", "ldg", "ve", "se", "te", "gp", "cc"):
            self.semh[name] = nc.alloc_semaphore(name=name)
            self.cnt[name] = 0
        self.wm = {s: {k: 0 for k in self.semh} for s in self.STREAMS}
        self.acc = {}
        self.retired = []
        self.born = {}

    def _names(self, aps):
        out = []
        for a in aps:
            if a is None or isinstance(a, (int, float)):
                continue
            t = getattr(a, "tensor", None)
            if t is None:
                t = a
            out.append(t.name)
        return out

    def _rec(self, name):
        if name not in self.acc:
            self.acc[name] = {"w": None, "r": []}
            self.born[name] = list(self.retired)
        return self.acc[name]

    def _deps(self, reads, writes):
        evs = []
        for n in reads:
            r = self._rec(n)
            if r["w"]:
                evs.append(r["w"])
            if self.born.get(n):
                evs.extend(self.born[n]); self.born[n] = []
        for n in writes:
            r = self._rec(n)
            if r["w"]:
                evs.append(r["w"])
            evs.extend(r["r"])
            if self.born.get(n):
                evs.extend(self.born[n]); self.born[n] = []
        ded = {}
        for s, v in evs:
            ded[s] = max(ded.get(s, 0), v)
        return ded

    def _commit(self, reads, writes, ev):
        for n in reads:
            self._rec(n)["r"].append(ev)
        for n in writes:
            r = self._rec(n)
            r["w"] = ev
            r["r"] = []

    def free(self, *handles):
        for h in handles:
            n = h.name if hasattr(h, "name") else h
            r = self.acc.get(n)
            if r:
                if r["w"]:
                    self.retired.append(r["w"])
                self.retired.extend(r["r"])
        ded = {}
        for s, v in self.retired:
            ded[s] = max(ded.get(s, 0), v)
        self.retired = [(s, v) for s, v in ded.items()]

    def _emit(self, stream, sem, inc, build, reads_aps, writes_aps, extra_ev=None):
        reads = self._names(reads_aps)
        writes = self._names(writes_aps)
        deps = self._deps(reads, writes)
        if extra_ev:
            for s, v in extra_ev:
                deps[s] = max(deps.get(s, 0), v)
        waits = []
        for s, v in deps.items():
            if self.wm[stream][s] < v:
                waits.append((self.semh[s], v))
                self.wm[stream][s] = v
        self.cnt[sem] += inc
        ev = (sem, self.cnt[sem])
        semh = self.semh[sem]

        def fn(eng, waits=waits, build=build, semh=semh, inc=inc):
            for h, v in waits:
                eng.wait_ge(h, v)
            ins = build(eng)
            if ins is not None:
                ins.then_inc(semh, inc)

        self.prog[stream].append(fn)
        self._commit(reads, writes, ev)
        return ev

    def dma(self, stream, out, in_, extra_ev=None, slow_ok=False):
        sem = {"sync": "lds", "scalar": "lda", "gpsimd": "ldg"}[stream]
        if slow_ok:
            return self._emit(stream, sem, 16,
                              lambda e: e.dma_start(out=out, in_=in_,
                                                    allow_slow_non_contiguous=True),
                              [in_], [out], extra_ev=extra_ev)
        return self._emit(stream, sem, 16,
                          lambda e: e.dma_start(out=out, in_=in_), [in_], [out],
                          extra_ev=extra_ev)

    def copy(self, out, in_, eng="vector"):
        out, in_ = _A(out), _A(in_)
        if eng == "scalar":
            return self._emit(eng, "se", 1,
                              lambda e: e.activation(out, in_, AF.Copy), [in_], [out])
        sem = {"vector": "ve", "gpsimd": "gp"}[eng]
        return self._emit(eng, sem, 1, lambda e: e.tensor_copy(out, in_), [in_], [out])

    def tt(self, out, a, b, op=AO.mult, eng="vector"):
        out, a, b = _A(out), _A(a), _A(b)
        sem = {"vector": "ve", "gpsimd": "gp"}[eng]
        return self._emit(eng, sem, 1, lambda e: e.tensor_tensor(out, a, b, op),
                          [a, b], [out])

    def ts(self, out, a, s1, s2=None, op0=AO.mult, op1=AO.bypass, eng="vector"):
        out, a, s1, s2 = _A(out), _A(a), _A(s1), _A(s2)
        sem = {"vector": "ve", "gpsimd": "gp"}[eng]
        rd = [a] + [x for x in (s1, s2) if hasattr(x, "tensor")]
        if s2 is None:
            return self._emit(eng, sem, 1,
                              lambda e: e.tensor_single_scalar(out, a, s1, op0),
                              rd, [out])
        return self._emit(eng, sem, 1,
                          lambda e: e.tensor_scalar(out, a, s1, s2, op0, op1),
                          rd, [out])

    def stt(self, out, in0, scalar, in1, op0, op1, eng="vector"):
        out, in0, scalar, in1 = _A(out), _A(in0), _A(scalar), _A(in1)
        sem = {"vector": "ve", "gpsimd": "gp"}[eng]
        rd = [in0, in1] + ([scalar] if hasattr(scalar, "tensor") else [])
        return self._emit(eng, sem, 1,
                          lambda e: e.scalar_tensor_tensor(out, in0, scalar, in1, op0, op1),
                          rd, [out])

    def scan(self, out, d0, d1, initial, op0, op1):
        out, d0, d1 = _A(out), _A(d0), _A(d1)
        rd = [d0, d1] + ([initial] if hasattr(initial, "tensor") else [])
        return self._emit("vector", "ve", 1,
                          lambda e: e.tensor_tensor_scan(out, d0, d1, initial, op0, op1),
                          rd, [out])

    def red(self, out, in_, op=AO.add, absval=False, negate=False):
        out, in_ = _A(out), _A(in_)
        def b(e):
            return e.tensor_reduce(out, in_, axis=AXX, op=op,
                                   apply_absolute_value=absval, negate=negate)
        return self._emit("vector", "ve", 1, b, [in_], [out])

    def recip(self, out, in_):
        out, in_ = _A(out), _A(in_)
        return self._emit("vector", "ve", 1, lambda e: e.reciprocal(out, in_),
                          [in_], [out])

    def act(self, out, in_, func, bias=0.0, scale=1.0):
        out, in_, bias, scale = _A(out), _A(in_), _A(bias), _A(scale)
        rd = [in_] + [x for x in (bias, scale) if hasattr(x, "tensor")]
        return self._emit("scalar", "se", 1,
                          lambda e: e.activation(out, in_, func, bias=bias, scale=scale),
                          rd, [out])

    def memset(self, ap, val, eng="vector"):
        ap = _A(ap)
        sem = {"vector": "ve", "gpsimd": "gp"}[eng]
        return self._emit(eng, sem, 1, lambda e: e.memset(ap, val), [], [ap])

    def mm(self, out, lhsT, rhs, start, stop, inc=True):
        out, lhsT, rhs = _A(out), _A(lhsT), _A(rhs)
        if not hasattr(self, "_g"):
            self._g = None
        def b(e, out=out, lhsT=lhsT, rhs=rhs, start=start, stop=stop):
            return e.matmul(out, lhsT, rhs, start=start, stop=stop)
        if self._g is None and start and stop:
            return self._emit("tensor", "te", 1, b, [lhsT, rhs], [out])
        if self._g is None:
            self._g = []
        self._g.append((b, [lhsT, rhs], [out]))
        if not stop:
            return None
        group = self._g
        self._g = None
        deps = {}
        for _, rd, wr in group:
            d = self._deps(self._names(rd), self._names(wr))
            for s, v in d.items():
                deps[s] = max(deps.get(s, 0), v)
        waits = []
        for s, v in deps.items():
            if self.wm["tensor"][s] < v:
                waits.append((self.semh[s], v))
                self.wm["tensor"][s] = v
        self.cnt["te"] += 1
        ev = ("te", self.cnt["te"])
        semh = self.semh["te"]

        def fn(eng, waits=waits, group=group, semh=semh):
            for h, v in waits:
                eng.wait_ge(h, v)
            ins = None
            for gb, _, _ in group:
                ins = gb(eng)
            ins.then_inc(semh, 1)

        self.prog["tensor"].append(fn)
        for _, rd, wr in group:
            self._commit(self._names(rd), self._names(wr), ev)
        return ev

    def transpose(self, out_psum, in_sb, ident):
        out_psum, in_sb, ident = _A(out_psum), _A(in_sb), _A(ident)
        kp = in_sb.shape[0]
        def b(e):
            return e.matmul(out_psum, in_sb, ident[0:kp, 0:kp], start=True, stop=True,
                            is_transpose=True)
        return self._emit("tensor", "te", 1, b, [in_sb, ident], [out_psum])

    def collective(self, kind, op, rg, ins, outs):
        return self._emit("gpsimd", "cc", 1,
                          lambda e: e.collective_compute(kind, op, rg, ins, outs),
                          ins, outs)


def build_program(debug=False, upto=99, asub=9):
    nc = bass.Bass("TRN2", target_bir_lowering=False, debug=False)
    S_ = Sched(nc)
    rg = [list(range(NCORE))]

    din = {}
    def IN(name, shape, dt=F32):
        din[name] = nc.dram_tensor(name, shape, dt, kind="ExternalInput").ap()
        return din[name]

    x_in = IN("x", [D, T])
    wqa_in = IN("wqa", [16, 128, 2048])
    wka_in = IN("wka", [4, 128, 2048])
    wva_in = IN("wva", [4, 128, 2048])
    woa_in = IN("woa", [16, 128, 2048])
    wqb_in = IN("wqb", [16, 128, 2048])
    wkb_in = IN("wkb", [16, 128, 2048])
    wvb_in = IN("wvb", [16, 128, 2048])
    wgb_in = IN("wgb", [16, 128, 2048])
    wob_in = IN("wob", [16, 128, 2048])
    wkg1_in = IN("wkg1t", [128, 256])
    wkg2_in = IN("wkg2p", [16, 2048])
    bkg2_in = IN("bkg2p", [128, 16])
    bg_in = IN("bgc", [128, 16])
    wg_in = IN("wgf", [NIC, 128, 2048])
    wu_in = IN("wuf", [NIC, 128, 2048])
    wd_in = IN("wdf", [16, 128, NIC * 128])
    wsl_in = IN("wsl", [128, 3 * 12928])
    ident_in = IN("ident", [128, 128], BF16)
    masku_in = IN("masku", [128, 128], BF16)
    prot_in = IN("prot", [128, 128], BF16)
    onecol_in = IN("onecol", [128, 1], BF16)
    onerow_in = IN("onerow", [1, 128], BF16)
    cos_in = IN("cosr", [128, T])
    sin_in = IN("sinr", [128, T])
    selw_in = IN("selw", [128, 8])
    out_d = nc.dram_tensor("out", [D, T], F32, kind="ExternalOutput").ap()
    dbg = {}
    if debug:
        for nm in ("d_ln1", "d_xa", "d_xf1", "d_xb"):
            dbg[nm] = nc.dram_tensor(nm, [D, T], F32, kind="ExternalOutput").ap()

    ar_in = nc.dram_tensor("ar_in", [1, 3], F32, kind="Internal").ap()
    ar_out = nc.dram_tensor("ar_out", [1, 3], F32, kind="Internal",
                            addr_space="Shared").ap()
    acc_d = nc.dram_tensor("acc_d", [128, 3], F32, kind="Internal").ap()
    scales_d = nc.dram_tensor("scales_d", [1, 9], F32, kind="Internal").ap()
    qdt = [nc.dram_tensor(f"qdt{i}", [1, 4, 128], F32, kind="Internal").ap()
           for i in range(3)]
    aga_in = nc.dram_tensor("aga_in", [128, 528], BF16, kind="Internal").ap()
    aga_out = nc.dram_tensor("aga_out", [NCORE, 128, 528], BF16, kind="Internal",
                             addr_space="Shared").ap()
    agb_in = nc.dram_tensor("agb_in", [64, 2048], BF16, kind="Internal").ap()
    agb_out = nc.dram_tensor("agb_out", [NCORE, 64, 2048], BF16, kind="Internal",
                             addr_space="Shared").ap()
    terng_d = nc.dram_tensor("terng_d", [NIC, 128, 2048], BF16, kind="Internal").ap()
    ternu_d = nc.dram_tensor("ternu_d", [NIC, 128, 2048], BF16, kind="Internal").ap()
    ternd_d = nc.dram_tensor("ternd_d", [16, 128, NIC * 128], BF16, kind="Internal").ap()
    kh_d = nc.dram_tensor("kh_d", [16, 128, T], BF16, kind="Internal").ap()
    gk_d = nc.dram_tensor("gk_d", [16, 128, T], BF16, kind="Internal").ap()
    vt_d = nc.dram_tensor("vt_d", [16, 128, T], BF16, kind="Internal").ap()

    big = ExitStack()
    def SB(name, shape, dt):
        return big.enter_context(nc.sbuf_tensor(name, shape, dt))

    xt = [SB(f"xt{i}", [128, T], F32) for i in range(NDC)]
    ln = [SB(f"ln{i}", [128, T], BF16) for i in range(NDC)]
    ident = SB("identS", [128, 128], BF16)
    masku = SB("maskuS", [128, 128], BF16)
    prot = SB("protS", [128, 128], BF16)
    onecol = SB("onecolS", [128, 1], BF16)
    onerow = SB("onerowS", [1, 128], BF16)
    cosb = SB("cosb", [128, T], BF16)
    sinb = SB("sinb", [128, T], BF16)
    selw = SB("selwS", [128, 8], F32)
    ccol = SB("ccol", [128, 8], F32)
    scol = SB("scol", [128, 12], F32)
    bkg2c = SB("bkg2cS", [128, 16], F32)
    bgc = SB("bgcS", [128, 16], F32)
    rows = SB("rows", [1, 2048], F32)
    rowb = SB("rowb", [1, 1024], BF16)
    ones32 = SB("ones32", [128, 128], F32)

    NHALF, PHALF, EPS5, EPS6, NEG1 = (ccol[:, i:i + 1] for i in range(5))

    pb = [big.enter_context(nc.psum_tensor(f"pb{i}", [128, 512], F32))
          for i in range(6)]
    pbA, pbB, pbC, pbD, pbE, pbF = pb
    ptA = big.enter_context(nc.psum_tensor("ptA", [128, 256], BF16))
    ptB = big.enter_context(nc.psum_tensor("ptB", [128, 256], BF16))

    S_.memset(ccol[:, 0:1], -0.5)
    S_.memset(ccol[:, 1:2], 0.5)
    S_.memset(ccol[:, 2:3], 1e-5)
    S_.memset(ccol[:, 3:4], 1e-6)
    S_.memset(ccol[:, 4:5], -1.0)
    S_.memset(ones32[:, :], 1.0)
    for i in range(NDC):
        S_.dma("sync", xt[i][:, :], x_in[i * 128:(i + 1) * 128, :])
    S_.dma("sync", ident[:, :], ident_in)
    S_.dma("sync", masku[:, :], masku_in)
    S_.dma("sync", prot[:, :], prot_in)
    S_.dma("sync", onecol[:, :], onecol_in)
    S_.dma("sync", onerow[:, :], onerow_in)
    S_.dma("sync", selw[:, :], selw_in)
    S_.dma("sync", bkg2c[:, :], bkg2_in)
    S_.dma("sync", bgc[:, :], bg_in)

    p0 = ExitStack()
    def P0(name, shape, dt):
        return p0.enter_context(nc.sbuf_tensor(name, shape, dt))
    cs32 = P0("cs32", [128, T], F32)
    S_.dma("sync", cs32[:, :], cos_in)
    S_.copy(cosb[:, :], cs32[:, :])
    S_.dma("sync", cs32[:, :], sin_in)
    S_.copy(sinb[:, :], cs32[:, :])

    wslb = [P0(f"wslb{i}", [128, 6464], F32) for i in range(2)]
    acc3 = P0("acc3", [128, 4], F32)
    rc1 = P0("rc1", [128, 1], F32)
    for mi in range(3):
        for hf in range(2):
            buf = wslb[hf]
            sl = slice(mi * 12928 + hf * 6464, mi * 12928 + (hf + 1) * 6464)
            S_.dma("scalar", buf[:, :], wsl_in[:, sl])
            if hf == 0:
                S_.red(acc3[:, mi:mi + 1], buf[:, :], AO.add, absval=True)
            else:
                S_.red(rc1[:, :], buf[:, :], AO.add, absval=True)
                S_.tt(acc3[:, mi:mi + 1], acc3[:, mi:mi + 1], rc1[:, :], AO.add)
    accrow = P0("accrow", [1, 384], F32)
    S_.dma("sync", acc_d, acc3[:, 0:3])
    S_.dma("sync", accrow[0:1, :], acc_d)
    for mi in range(3):
        S_.red(rows[0:1, 16 + mi:17 + mi], accrow[0:1, mi::3], AO.add)
    S_.dma("sync", ar_in, rows[0:1, 16:19])
    S_.collective("AllReduce", AO.add, rg, [ar_in], [ar_out])
    S_.dma("sync", rows[0:1, 0:3], ar_out)
    S_.ts(rows[0:1, 8:11], rows[0:1, 0:3], 1.0 / FFN_CNT, 1e-5, AO.mult, AO.max)
    S_.recip(rows[0:1, 11:14], rows[0:1, 8:11])
    S_.ts(rows[0:1, 14:17], rows[0:1, 8:11], 0.5, None, AO.mult)
    S_.dma("sync", scales_d, rows[0:1, 8:17])
    S_.dma("sync", scol[:, 0:9], scales_d.broadcast_to([128, 9]))
    p0.close()
    S_.free("cs32", "wslb0", "wslb1", "acc3", "rc1", "accrow")

    lnscr = [SB(f"lnscr{i}", [128, 512], BF16) for i in range(2)]
    ln_n = [0]

    def layernorm():
        ln_n[0] += 1
        les = ExitStack()
        sqt = [les.enter_context(nc.sbuf_tensor(f"lnsq{ln_n[0]}_{i}", [128, 512], BF16))
               for i in range(NDC)]
        for tt in range(2):
            tsl = slice(tt * TW, (tt + 1) * TW)
            for i in range(NDC):
                S_.copy(ln[i][:, tsl], xt[i][:, tsl],
                        eng=("vector" if i % 2 == 0 else "vector"))
            for i in range(NDC):
                S_.mm(pbC[0:1, 0:512], onecol, ln[i][:, tsl],
                      start=(i == 0), stop=(i == NDC - 1), inc=(i == NDC - 1))
            for i in range(NDC):
                S_.act(sqt[i][:, :], ln[i][:, tsl], AF.Square)
            for i in range(NDC):
                S_.mm(pbD[0:1, 0:512], onecol, sqt[i][:, :],
                      start=(i == 0), stop=(i == NDC - 1), inc=(i == NDC - 1))
            mu = rows[0:1, 0:512]
            musq = rows[0:1, 512:1024]
            var = rows[0:1, 1024:1536]
            rstd = rows[0:1, 1536:2048]
            S_.ts(mu, pbC[0:1, 0:512], 1.0 / D, None, AO.mult)
            S_.tt(musq, mu, mu, AO.mult)
            S_.stt(var, pbD[0:1, 0:512], 1.0 / D, musq, AO.mult, AO.subtract)
            S_.act(var, var, AF.Sqrt, bias=ccol[0:1, 2:3])
            S_.recip(rstd, var)
            S_.copy(rowb[0:1, 0:512], rstd)
            S_.stt(musq, mu, -1.0, rstd, AO.mult, AO.mult)
            S_.copy(rowb[0:1, 512:1024], musq)
            S_.mm(pbC[:, 0:512], onerow, rowb[0:1, 0:512], start=True, stop=True)
            S_.mm(pbD[:, 0:512], onerow, rowb[0:1, 512:1024], start=True, stop=True)
            S_.ts(lnscr[0][:, :], pbC[0:128, 0:512], 1.0, None, AO.mult)
            S_.ts(lnscr[1][:, :], pbD[0:128, 0:512], 1.0, None, AO.mult)
            for i in range(NDC):
                S_.tt(ln[i][:, tsl], ln[i][:, tsl], lnscr[0][:, :], AO.mult)
                S_.tt(ln[i][:, tsl], ln[i][:, tsl], lnscr[1][:, :], AO.add)
        les.close()
        S_.free(*[f"lnsq{ln_n[0]}_{i}" for i in range(NDC)])

    def project(strips_ap, n_oc, consumer, bufs, bank_of=None, rhs=None, tts=(0, 1)):
        rhs = rhs or ln
        for oc in range(n_oc):
            buf = bufs[oc % len(bufs)]
            S_.dma("gpsimd", buf[:, :], strips_ap[oc])
            for tt in tts:
                tsl = slice(tt * TW, (tt + 1) * TW)
                bank = (bank_of((oc * len(tts) + (tt - tts[0])) % 2) if bank_of
                        else pb[(oc * len(tts) + tt) % 2])
                for dc in range(NDC):
                    S_.mm(bank[:, 0:512], buf[:, dc * 128:(dc + 1) * 128],
                          rhs[dc][:, tsl], start=(dc == 0), stop=(dc == NDC - 1),
                          inc=(dc == NDC - 1))
                consumer(oc, tt, bank)

    if upto >= 1:
        layernorm()

    if upto >= 1:
        aa = ExitStack()
        def AA(name, shape, dt):
            return aa.enter_context(nc.sbuf_tensor(name, shape, dt))

        qf = [AA(f"qf{h}", [128, T], BF16) for h in range(16)]
        kf = [AA(f"kf{g}", [128, T], BF16) for g in range(4)]
        vtm = [AA(f"vtm{g}", [128, NCH * 132], BF16) for g in range(4)]
        srun = AA("srun", [128, 4 * 132], F32)
        sbfa = AA("sbfa", [128, 4 * 132], BF16)
        rsc = [AA(f"rsc{i}", [128, 512], BF16) for i in range(6)]
        ktm = AA("ktm", [128, 128], BF16)
        ktm2 = AA("ktm2", [128, 128], BF16)
        strips = [AA(f"strip{i}", [128, 2048], BF16) for i in range(3)]

        def rope_elu(dst, oc, tt, bank):
            tsl = slice(tt * TW, (tt + 1) * TW)
            S_.act(rsc[0][:, :], bank[:, 0:512], AF.Copy)
            S_.mm(pbC[:, 0:512], prot, rsc[0][:, :], start=True, stop=True)
            S_.tt(rsc[1][:, :], rsc[0][:, :], cosb[:, tsl], AO.mult)
            S_.stt(rsc[2][:, :], pbC[0:128, 0:512], 1.0, sinb[:, tsl], AO.mult, AO.mult)
            S_.tt(rsc[2][:, :], rsc[2][:, :], rsc[1][:, :], AO.add)
            S_.ts(rsc[3][:, :], rsc[2][:, :], 0.0, None, AO.min)
            S_.act(rsc[4][:, :], rsc[3][:, :], AF.Exp)
            S_.stt(dst[oc][:, tsl], rsc[2][:, :], 0.0, rsc[4][:, :], AO.max, AO.add)

        if asub >= 1:
            project(wka_in, 4, lambda oc, tt, bank: rope_elu(kf, oc, tt, bank), strips)

        for g in range(4):
            S_.memset(vtm[g][:, :], 1.0)

        def v_consume(oc, tt, bank):
            S_.act(rsc[5][:, :], bank[:, 0:512], AF.Copy)
            for j in range(4):
                c = tt * 4 + j
                ptj = ptA if j % 2 == 0 else ptB
                S_.transpose(ptj[:, 0:128], rsc[5][:, j * 128:(j + 1) * 128], ident)
                S_.copy(vtm[oc][:, c * 132:c * 132 + 128], ptj[0:128, 0:128], eng="scalar")
        if asub >= 1:
            project(wva_in, 4, v_consume, strips)
            project(wqa_in, 16, lambda oc, tt, bank: rope_elu(qf, oc, tt, bank), strips)

        def a_state_mm(g, c):
            csl = slice(c * CH, (c + 1) * CH)
            ptj = ptA if g % 2 == 0 else ptB
            ktmj = ktm if g % 2 == 0 else ktm2
            S_.transpose(ptj[:, 128:256], kf[g][:, csl], ident)
            S_.copy(ktmj[:, :], ptj[0:128, 128:256], eng="scalar")
            S_.mm(pbF[:, 0:129], ktmj, vtm[g][:, c * 132:c * 132 + 129],
                  start=True, stop=True)

        if asub >= 2:
            for c in range(NCH):
                for g in range(4):
                    a_state_mm(g, c)
                    ssl = slice(g * 132, g * 132 + 129)
                    if c == 0:
                        S_.ts(srun[:, ssl], pbF[0:128, 0:129], 1.0, None, AO.mult)
                    else:
                        S_.tt(srun[:, ssl], srun[:, ssl], pbF[0:128, 0:129], AO.add)
            S_.copy(sbfa[:, :], srun[:, :])
        else:
            S_.memset(srun[:, :], 0.0)
            S_.copy(sbfa[:, :], srun[:, :])
        if asub >= 3:
            S_.dma("sync", aga_in, sbfa[:, 0:528])
            S_.collective("AllGather", AO.bypass, rg, [aga_in], [aga_out])

        agr = [AA(f"agr{i}", [128, 528], BF16) for i in range(2)]
        for r in (range(NCORE) if asub >= 3 else []):
            buf = agr[r % 2]
            S_.dma("sync", buf[:, :], aga_out[r])
            if r == 0:
                S_.ts(srun[:, :], buf[:, :], selw[:, 0:1], None, AO.mult)
            else:
                S_.stt(srun[:, :], buf[:, :], selw[:, r:r + 1], srun[:, :],
                       AO.mult, AO.add)
        S_.copy(sbfa[:, :], srun[:, :])

        for c in range(NCH):
            csl = slice(c * CH, (c + 1) * CH)
            for g in range(4):
                hs = [4 * g + i for i in range(4)]
                for i, h in enumerate(hs):
                    S_.mm(pbD[:, i * 128:(i + 1) * 128], kf[g][:, csl], qf[h][:, csl],
                          start=(i == 0), stop=(i == 3), inc=(i == 3))
                for i, h in enumerate(hs):
                    S_.tt(rsc[i][:, 0:128], pbD[0:128, i * 128:(i + 1) * 128],
                          masku, AO.mult)
                for i, h in enumerate(hs):
                    S_.mm(pbE[:, i * 128:(i + 1) * 128], vtm[g][:, c * 132:c * 132 + 128],
                          rsc[i][:, 0:128], start=(i == 0), stop=False, inc=False)
                    S_.mm(pbE[:, i * 128:(i + 1) * 128], sbfa[:, g * 132:g * 132 + 128],
                          qf[h][:, csl], start=False, stop=(i == 3), inc=(i == 3))
                for i, h in enumerate(hs):
                    S_.mm(pbF[0:1, i * 128:(i + 1) * 128], onecol, rsc[i][:, 0:128],
                          start=(i == 0), stop=False, inc=False)
                    S_.mm(pbF[0:1, i * 128:(i + 1) * 128],
                          sbfa[:, g * 132 + 128:g * 132 + 129], qf[h][:, csl],
                          start=False, stop=(i == 3), inc=(i == 3))
                S_.ts(rows[0:1, 0:512], pbF[0:1, 0:512], 1e-6, None, AO.add)
                S_.recip(rows[0:1, 512:1024], rows[0:1, 0:512])
                S_.copy(rowb[0:1, 0:512], rows[0:1, 512:1024])
                S_.mm(pbC[:, 0:512], onerow, rowb[0:1, 0:512], start=True, stop=True)
                S_.ts(rsc[4][:, :], pbC[0:128, 0:512], 1.0, None, AO.mult)
                for i, h in enumerate(hs):
                    S_.tt(qf[h][:, csl], pbE[0:128, i * 128:(i + 1) * 128],
                          rsc[4][:, i * 128:(i + 1) * 128], AO.mult)
            if c < NCH - 1:
                for g in range(4):
                    a_state_mm(g, c)
                    ssl = slice(g * 132, g * 132 + 129)
                    S_.tt(srun[:, ssl], srun[:, ssl], pbF[0:128, 0:129], AO.add)
                    S_.copy(sbfa[:, ssl], srun[:, ssl])

        def woa_consume(oc, tt, bank):
            tsl = slice(tt * TW, (tt + 1) * TW)
            S_.stt(xt[oc][:, tsl], bank[:, 0:512], 1.0, xt[oc][:, tsl], AO.mult, AO.add)
        if asub >= 5:
            project(woa_in, 16, woa_consume, strips, rhs=qf)

        aa.close()
        S_.free(*[f"qf{h}" for h in range(16)], *[f"kf{g}" for g in range(4)],
                *[f"vtm{g}" for g in range(4)], "srun", "sbfa",
                *[f"rsc{i}" for i in range(6)], "ktm", "strip0", "strip1",
                "agr0", "agr1")
        if debug:
            for i in range(NDC):
                S_.dma("sync", dbg["d_xa"][i * 128:(i + 1) * 128, :], xt[i][:, :])

    def ffn(first):
        layernorm()
        fs = ExitStack()
        pfx = "fA_" if first else "fB_"
        def FF(name, shape, dt):
            return fs.enter_context(nc.sbuf_tensor(pfx + name, shape, dt))
        hbuf = [FF(f"h{ic}", [128, 512], BF16) for ic in range(NIC)]
        fsc = [FF(f"fsc{i}", [128, 512], BF16) for i in range(2)]
        hgt = FF("hgt", [128, 512], BF16)
        hut = FF("hut", [128, 512], BF16)
        sxb = FF("sxb", [128, 512], F32)
        fb = [FF(f"fb{i}", [128, 512], F32) for i in range(2)]
        q32 = FF("q32", [128, 512], F32)
        qi = FF("qi", [128, 512], I32)
        nb = 2 if first else 3
        if first:
            wf32 = [FF(f"wf32{i}", [128, 1024], F32) for i in range(2)]
            sg2 = [FF(f"sg2{i}", [128, 1024], BF16) for i in range(2)]
        terng = [FF(f"terng{i}", [128, 2048], BF16) for i in range(nb)]
        ternu = [FF(f"ternu{i}", [128, 2048], BF16) for i in range(nb)]

        def quant_rows(tiles, n, scis, outs):
            S_.tt(fsc[0][:, :], tiles[0], tiles[1], AO.max)
            S_.tt(fsc[1][:, :], tiles[0], tiles[1], AO.min)
            for i in range(2, n):
                S_.tt(fsc[0][:, :], fsc[0][:, :], tiles[i], AO.max)
                S_.tt(fsc[1][:, :], fsc[1][:, :], tiles[i], AO.min)
            S_.ts(fsc[1][:, :], fsc[1][:, :], -1.0, None, AO.mult)
            S_.tt(fsc[0][:, :], fsc[0][:, :], fsc[1][:, :], AO.max)
            for j in range(4):
                ptj = ptA if j % 2 == 0 else ptB
                S_.transpose(ptj[:, (j // 2) * 128:(j // 2) * 128 + 128],
                             fsc[0][:, j * 128:(j + 1) * 128], ident)
            for j in range(4):
                ptj = ptA if j % 2 == 0 else ptB
                S_.red(q32[:, j:j + 1], ptj[0:128, (j // 2) * 128:(j // 2) * 128 + 128],
                       AO.max)
            S_.ts(q32[:, 4:8], q32[:, 0:4], 1e-5, 1.0 / 127.0, AO.max, AO.mult)
            S_.recip(q32[:, 8:12], q32[:, 4:8])
            S_.dma("sync", qdt[0].rearrange("a j p -> p (a j)"), q32[:, 8:12], slow_ok=True)
            for i, sci in enumerate(scis):
                S_.ts(q32[:, 12 + 4 * i:16 + 4 * i], q32[:, 4:8],
                      scol[:, 6 + sci:7 + sci], None, AO.mult)
                S_.dma("sync", qdt[1 + i].rearrange("a j p -> p (a j)"),
                       q32[:, 12 + 4 * i:16 + 4 * i], slow_ok=True)
            for i, o in enumerate(outs):
                S_.dma("sync", o[:, :],
                       qdt[i].rearrange("a j p -> a (j p)").broadcast_to([128, 512]))

        def quantize(tile_ap):
            S_.tt(q32[:, :], tile_ap, sxb[:, :], AO.mult)
            S_.act(qi[:, :], q32[:, :], AF.Copy)
            S_.copy(tile_ap, qi[:, :])

        def tern_load(dst, src_ap, width, sci):
            nh = (width + 1023) // 1024
            for hf in range(nh):
                w = min(1024, width - hf * 1024)
                buf = wf32[hf % 2]
                S_.dma("scalar", buf[:, 0:w], src_ap[:, hf * 1024:hf * 1024 + w])
                S_.act(sg2[0][:, 0:w], buf[:, 0:w], AF.Sign, bias=NHALF,
                       scale=scol[:, 3 + sci:4 + sci])
                S_.act(sg2[1][:, 0:w], buf[:, 0:w], AF.Sign, bias=PHALF,
                       scale=scol[:, 3 + sci:4 + sci])
                S_.tt(dst[:, hf * 1024:hf * 1024 + w], sg2[0][:, 0:w],
                      sg2[1][:, 0:w], AO.add)

        for tt in range(2):
            tsl = slice(tt * TW, (tt + 1) * TW)
            lslices = [ln[i][:, tsl] for i in range(NDC)]
            quant_rows(lslices, NDC, (0, 1), (sxb, fb[0], fb[1]))
            for i in range(NDC):
                quantize(ln[i][:, tsl])
            for oc in range(NIC):
                tg, tu = terng[oc % nb], ternu[oc % nb]
                if first and tt == 0:
                    tern_load(tg, wg_in[oc], 2048, 0)
                    S_.dma("sync", terng_d[oc], tg[:, :])
                    tern_load(tu, wu_in[oc], 2048, 1)
                    S_.dma("sync", ternu_d[oc], tu[:, :])
                else:
                    S_.dma("scalar" if oc % 2 == 0 else "sync", tg[:, :], terng_d[oc])
                    S_.dma("scalar" if oc % 2 == 1 else "sync", tu[:, :], ternu_d[oc])
                bg_, bu_ = [(pbA, pbB), (pbC, pbD), (pbE, pbF)][oc % 3]
                for dc in range(NDC):
                    S_.mm(bg_[:, 0:512], tg[:, dc * 128:(dc + 1) * 128],
                          ln[dc][:, tsl], start=(dc == 0), stop=(dc == NDC - 1),
                          inc=(dc == NDC - 1))
                for dc in range(NDC):
                    S_.mm(bu_[:, 0:512], tu[:, dc * 128:(dc + 1) * 128],
                          ln[dc][:, tsl], start=(dc == 0), stop=(dc == NDC - 1),
                          inc=(dc == NDC - 1))
                S_.tt(q32[:, :], bg_[0:128, 0:512], fb[0][:, :], AO.mult)
                S_.act(hgt[:, :], q32[:, :], AF.Silu)
                S_.tt(hut[:, :], bu_[0:128, 0:512], fb[1][:, :], AO.mult)
                S_.tt(hbuf[oc][:, :], hgt[:, :], hut[:, :], AO.mult)
            hs = [hbuf[i][:, :] for i in range(NIC)]
            quant_rows(hs, NIC, (2,), (sxb, fb[0]))
            for ic in range(NIC):
                quantize(hbuf[ic][:, :])
            for oc in range(NDC):
                bd_ = pbE if oc % 2 == 0 else pbF
                for qtr in range(4):
                    k0 = qtr * 13
                    nk = 13 if qtr < 3 else NIC - 39
                    buf = [terng[0], ternu[0], terng[1], ternu[1]][qtr]
                    if first and tt == 0:
                        tern_load(buf[:, 0:nk * 128],
                                  wd_in[oc][:, k0 * 128:(k0 + nk) * 128],
                                  nk * 128, 2)
                        S_.dma("sync", ternd_d[oc][:, k0 * 128:(k0 + nk) * 128],
                               buf[:, 0:nk * 128])
                    else:
                        S_.dma("scalar", buf[:, 0:nk * 128],
                               ternd_d[oc][:, k0 * 128:(k0 + nk) * 128])
                    for k in range(nk):
                        kc = k0 + k
                        S_.mm(bd_[:, 0:512], buf[:, k * 128:(k + 1) * 128],
                              hbuf[kc][:, :], start=(kc == 0), stop=(kc == NIC - 1),
                              inc=(kc == NIC - 1))
                S_.tt(q32[:, :], bd_[0:128, 0:512], fb[0][:, :], AO.mult)
                S_.tt(xt[oc][:, tsl], xt[oc][:, tsl], q32[:, :], AO.add)
        fs.close()
        S_.free(*[pfx + f"h{ic}" for ic in range(NIC)],
                *[pfx + f"fsc{i}" for i in range(2)],
                pfx + "hgt", pfx + "hut", pfx + "sxb", pfx + "fb0", pfx + "fb1",
                pfx + "q32", pfx + "qi",
                *[pfx + f"terng{i}" for i in range(nb)],
                *[pfx + f"ternu{i}" for i in range(nb)],
                *([pfx + "sg20", pfx + "sg21", pfx + "wf320", pfx + "wf321"]
                  if first else []))

    if upto >= 2:
        ffn(True)
    if debug:
        for i in range(NDC):
            S_.dma("sync", dbg["d_xf1"][i * 128:(i + 1) * 128, :], xt[i][:, :])

    if upto >= 3:
        layernorm()
        H = 16

        b3 = ExitStack()
        def B3(name, shape, dt):
            return b3.enter_context(nc.sbuf_tensor(name, shape, dt))
        sgla = B3("sgla", [128, H * 128], F32)
        sglab = B3("sglab", [128, H * 128], BF16)
        egl = B3("egl", [128, H * NCH], F32)
        G32p = [B3(f"G32_{i}", [128, 128], F32) for i in range(2)]
        gchp = [B3(f"gch_{i}", [128, 128], F32) for i in range(2)]
        e32p = [[B3(f"e32{i}_{j}", [128, 128], BF16) for i in range(4)]
                for j in range(2)]
        mcolp = [B3(f"mcol_{i}", [128, 4], F32) for i in range(2)]
        btmp = [B3(f"btm_{i}", [128, 128], BF16) for i in range(2)]

        S_.memset(sgla[:, :], 0.0)

        for hh in range(2):
            hsl_t = slice(hh * 512, (hh + 1) * 512)
            b1 = ExitStack()
            def B1(name, shape, dt):
                return b1.enter_context(nc.sbuf_tensor(f"p1h{hh}_" + name, shape, dt))
            khb = [B1(f"khb{h}", [128, 512], BF16) for h in range(H)]
            gkb = [B1(f"gkb{h}", [128, 512], BF16) for h in range(H)]
            vtb = [B1(f"vtb{h}", [128, 512], BF16) for h in range(H)]
            bstr = [B1(f"bstr{i}", [128, 2048], BF16) for i in range(2)]
            kg1 = B1("kg1", [16, 512], BF16)
            wkg1s = B1("wkg1s", [128, 256], BF16)
            wkg2s = B1("wkg2s", [16, 2048], BF16)
            sg2v = B1("sg2v", [128, 512], BF16)

            def k_consume(oc, tt, bank):
                S_.ts(khb[oc][:, :], bank[:, 0:512], float(128.0 ** -0.5), None, AO.mult)
            project(wkb_in, 16, k_consume, bstr, tts=(hh,))

            def vb_consume(oc, tt, bank):
                S_.act(sg2v[:, :], bank[:, 0:512], AF.Copy)
                for j in range(4):
                    ptj = ptA if j % 2 == 0 else ptB
                    S_.transpose(ptj[:, 0:128], sg2v[:, j * 128:(j + 1) * 128], ident)
                    S_.copy(vtb[oc][:, j * 128:(j + 1) * 128], ptj[0:128, 0:128],
                            eng="scalar")
            project(wvb_in, 16, vb_consume, bstr, tts=(hh,))

            S_.dma("gpsimd", wkg1s[:, :], wkg1_in)
            S_.dma("gpsimd", wkg2s[:, :], wkg2_in)
            for dc in range(NDC):
                S_.mm(pbC[0:16, 0:512], wkg1s[:, dc * 16:(dc + 1) * 16],
                      ln[dc][:, hsl_t], start=(dc == 0), stop=(dc == NDC - 1),
                      inc=(dc == NDC - 1))
            S_.copy(kg1[:, :], pbC[0:16, 0:512])
            for h in range(H):
                S_.mm(pbD[:, 0:512], wkg2s[:, h * 128:(h + 1) * 128], kg1[:, :],
                      start=True, stop=True)
                S_.act(sg2v[:, :], pbD[0:128, 0:512], AF.Sigmoid, bias=bkg2c[:, h:h + 1])
                S_.act(gkb[h][:, :], sg2v[:, :], AF.Ln, scale=1.0)

            for cc in range(4):
                c = hh * 4 + cc
                csl = slice(cc * CH, (cc + 1) * CH)
                for h in range(H):
                    hsl = slice(h * 128, (h + 1) * 128)
                    G32, gch, e32 = G32p[h % 2], gchp[h % 2], e32p[h % 2]
                    mcol, btm = mcolp[h % 2], btmp[h % 2]
                    S_.ts(gch[:, :], gkb[h][:, csl], 1.0 / 16.0, None, AO.mult)
                    S_.scan(G32[:, :], ones32, gch, 0.0, AO.mult, AO.add)
                    S_.ts(mcol[:, 0:1], G32[:, 127:128], 1.0, None, AO.mult)
                    S_.act(egl[:, h * NCH + c:h * NCH + c + 1], mcol[:, 0:1], AF.Exp)
                    S_.act(e32[3][:, :], G32[:, :], AF.Exp, bias=mcol[:, 0:1], scale=NEG1)
                    S_.tt(e32[3][:, :], khb[h][:, csl], e32[3][:, :], AO.mult)
                    ptj = ptA if h % 2 == 0 else ptB
                    S_.transpose(ptj[:, 0:128], e32[3][:, :], ident)
                    S_.copy(btm[:, :], ptj[0:128, 0:128], eng="scalar")
                    bst = pbE if h % 2 == 0 else pbF
                    S_.mm(bst[:, 0:128], btm, vtb[h][:, csl], start=True, stop=True)
                    S_.ts(sgla[:, hsl], sgla[:, hsl],
                          egl[:, h * NCH + c:h * NCH + c + 1], None, AO.mult)
                    S_.tt(sgla[:, hsl], sgla[:, hsl], bst[0:128, 0:128], AO.add)

            for h in range(H):
                S_.dma("scalar", kh_d[h][:, hsl_t], khb[h][:, :])
                S_.dma("scalar", gk_d[h][:, hsl_t], gkb[h][:, :])
                S_.dma("scalar", vt_d[h][:, hsl_t], vtb[h][:, :])
            b1.close()
            S_.free(*[f"p1h{hh}_khb{h}" for h in range(H)],
                    *[f"p1h{hh}_gkb{h}" for h in range(H)],
                    *[f"p1h{hh}_vtb{h}" for h in range(H)],
                    f"p1h{hh}_bstr0", f"p1h{hh}_bstr1", f"p1h{hh}_kg1",
                    f"p1h{hh}_wkg1s", f"p1h{hh}_wkg2s", f"p1h{hh}_sg2v")

        S_.copy(sglab[:, :], sgla[:, :])
        S_.dma("sync", agb_in, sglab[0:64, :])
        S_.collective("AllGather", AO.bypass, rg, [agb_in], [agb_out])
        ba = ExitStack()
        agrb = [ba.enter_context(nc.sbuf_tensor(f"agrb{i}", [64, 2048], BF16))
                for i in range(2)]
        for r in range(NCORE):
            buf = agrb[r % 2]
            S_.dma("sync", buf[:, :], agb_out[r])
            if r == 0:
                S_.ts(sgla[0:64, :], buf[:, :], selw[0:64, 0:1], None, AO.mult)
            else:
                S_.stt(sgla[0:64, :], buf[:, :], selw[0:64, r:r + 1], sgla[0:64, :],
                       AO.mult, AO.add)
        S_.memset(sgla[64:128, :], 0.0)
        S_.copy(sglab[:, :], sgla[:, :])
        ba.close()
        S_.free("agrb0", "agrb1")

        for hh in range(2):
            hsl_t = slice(hh * 512, (hh + 1) * 512)
            b2 = ExitStack()
            def B2(name, shape, dt):
                return b2.enter_context(nc.sbuf_tensor(f"p2h{hh}_" + name, shape, dt))
            qh2 = [B2(f"qh2_{h}", [128, 512], BF16) for h in range(H)]
            kh2 = [B2(f"kh2_{h}", [128, 512], BF16) for h in range(H)]
            gk2 = [B2(f"gk2_{h}", [128, 512], BF16) for h in range(H)]
            vt2 = [B2(f"vt2_{h}", [128, 512], BF16) for h in range(H)]
            bstr2 = [B2(f"bstr2_{i}", [128, 2048], BF16) for i in range(2)]
            scr2 = [B2(f"scr2_{i}", [128, 128], BF16) for i in range(4)]
            gvt = B2("gvt", [128, 512], BF16)
            sq2 = B2("sq2", [128, 512], BF16)

            for h in range(H):
                S_.dma("scalar", kh2[h][:, :], kh_d[h][:, hsl_t])
                S_.dma("scalar", gk2[h][:, :], gk_d[h][:, hsl_t])
                S_.dma("scalar", vt2[h][:, :], vt_d[h][:, hsl_t])

            def q_consume(oc, tt, bank):
                S_.ts(qh2[oc][:, :], bank[:, 0:512], 1.0, None, AO.mult)
            project(wqb_in, 16, q_consume, bstr2, tts=(hh,))

            for cc in range(4):
                c = hh * 4 + cc
                csl = slice(cc * CH, (cc + 1) * CH)
                for h in range(H):
                    hsl = slice(h * 128, (h + 1) * 128)
                    G32, gch, e32 = G32p[h % 2], gchp[h % 2], e32p[h % 2]
                    mcol, btm = mcolp[h % 2], btmp[h % 2]
                    sc0, sc1 = scr2[h % 2 * 2], scr2[h % 2 * 2 + 1]
                    bA = pbC if h % 2 == 0 else pbD
                    S_.ts(gch[:, :], gk2[h][:, csl], 1.0 / 16.0, None, AO.mult)
                    S_.scan(G32[:, :], ones32, gch, 0.0, AO.mult, AO.add)
                    S_.ts(mcol[:, 0:1], G32[:, 127:128], 1.0, None, AO.mult)
                    S_.act(egl[:, h * NCH + c:h * NCH + c + 1], mcol[:, 0:1], AF.Exp)
                    S_.act(e32[3][:, :], G32[:, :], AF.Exp, bias=mcol[:, 0:1], scale=NEG1)
                    S_.ts(mcol[:, 1:2], G32[:, 127:128], 0.5, None, AO.mult)
                    S_.ts(mcol[:, 2:3], mcol[:, 1:2], -1.0, None, AO.mult)
                    S_.act(e32[0][:, :], G32[:, :], AF.Exp, bias=mcol[:, 2:3])
                    S_.act(e32[1][:, :], G32[:, :], AF.Exp, bias=mcol[:, 1:2], scale=NEG1)
                    S_.act(e32[2][:, :], G32[:, :], AF.Exp)
                    S_.tt(sc0[:, :], qh2[h][:, csl], e32[0][:, :], AO.mult)
                    S_.tt(sc1[:, :], kh2[h][:, csl], e32[1][:, :], AO.mult)
                    S_.mm(bA[:, 0:128], sc1, sc0, start=True, stop=True)
                    S_.tt(sc0[:, :], bA[0:128, 0:128], masku, AO.mult)
                    S_.tt(sc1[:, :], qh2[h][:, csl], e32[2][:, :], AO.mult)
                    S_.mm(bA[:, 128:256], vt2[h][:, csl], sc0, start=True,
                          stop=False, inc=False)
                    S_.mm(bA[:, 128:256], sglab[:, hsl], sc1, start=False,
                          stop=True, inc=True)
                    S_.copy(qh2[h][:, csl], bA[0:128, 128:256])
                    S_.tt(e32[3][:, :], kh2[h][:, csl], e32[3][:, :], AO.mult)
                    ptj = ptA if h % 2 == 0 else ptB
                    S_.transpose(ptj[:, 0:128], e32[3][:, :], ident)
                    S_.copy(btm[:, :], ptj[0:128, 0:128], eng="scalar")
                    bK = pbE if h % 2 == 0 else pbF
                    S_.mm(bK[:, 0:128], btm, vt2[h][:, csl], start=True, stop=True)
                    S_.ts(sgla[:, hsl], sgla[:, hsl],
                          egl[:, h * NCH + c:h * NCH + c + 1], None, AO.mult)
                    S_.tt(sgla[:, hsl], sgla[:, hsl], bK[0:128, 0:128], AO.add)
                    S_.copy(sglab[:, hsl], sgla[:, hsl])

            for h in range(H):
                S_.act(sq2[:, :], qh2[h][:, :], AF.Square)
                S_.mm(pbC[0:1, 0:512], onecol, qh2[h][:, :], start=True, stop=True)
                S_.mm(pbD[0:1, 0:512], onecol, sq2[:, :], start=True, stop=True)
                mu = rows[0:1, 0:512]
                musq = rows[0:1, 512:1024]
                var = rows[0:1, 1024:1536]
                rstd = rows[0:1, 1536:2048]
                S_.ts(mu, pbC[0:1, 0:512], 1.0 / 128.0, None, AO.mult)
                S_.tt(musq, mu, mu, AO.mult)
                S_.stt(var, pbD[0:1, 0:512], 1.0 / 128.0, musq, AO.mult, AO.subtract)
                S_.act(var, var, AF.Sqrt, bias=ccol[0:1, 2:3])
                S_.recip(rstd, var)
                S_.copy(rowb[0:1, 0:512], rstd)
                S_.stt(musq, mu, -1.0, rstd, AO.mult, AO.mult)
                S_.copy(rowb[0:1, 512:1024], musq)
                S_.mm(pbC[:, 0:512], onerow, rowb[0:1, 0:512], start=True, stop=True)
                S_.mm(pbD[:, 0:512], onerow, rowb[0:1, 512:1024], start=True, stop=True)
                S_.ts(sq2[:, :], pbC[0:128, 0:512], 1.0, None, AO.mult)
                S_.tt(qh2[h][:, :], qh2[h][:, :], sq2[:, :], AO.mult)
                S_.ts(sq2[:, :], pbD[0:128, 0:512], 1.0, None, AO.mult)
                S_.tt(qh2[h][:, :], qh2[h][:, :], sq2[:, :], AO.add)

            def g_consume(oc, tt, bank):
                S_.act(gvt[:, :], bank[:, 0:512], AF.Silu, bias=bgc[:, oc:oc + 1])
                S_.tt(qh2[oc][:, :], qh2[oc][:, :], gvt[:, :], AO.mult)
            project(wgb_in, 16, g_consume, bstr2, tts=(hh,))

            def wob_consume(oc, tt, bank):
                S_.stt(xt[oc][:, hsl_t], bank[:, 0:512], 1.0, xt[oc][:, hsl_t],
                       AO.mult, AO.add)
            project(wob_in, 16, wob_consume, bstr2, rhs=qh2, tts=(0,))

            b2.close()
            S_.free(*[f"p2h{hh}_qh2_{h}" for h in range(H)],
                    *[f"p2h{hh}_kh2_{h}" for h in range(H)],
                    *[f"p2h{hh}_gk2_{h}" for h in range(H)],
                    *[f"p2h{hh}_vt2_{h}" for h in range(H)],
                    f"p2h{hh}_bstr2_0", f"p2h{hh}_bstr2_1", f"p2h{hh}_scr2_0",
                    f"p2h{hh}_scr2_1", f"p2h{hh}_gvt", f"p2h{hh}_sq2")

        b3.close()
        S_.free("sgla", "sglab", "egl",
                *[f"G32_{i}" for i in range(2)], *[f"gch_{i}" for i in range(2)],
                *[f"e32{i}_{j}" for i in range(4) for j in range(2)],
                *[f"mcol_{i}" for i in range(2)], *[f"btm_{i}" for i in range(2)])

    if debug:
        for i in range(NDC):
            S_.dma("sync", dbg["d_xb"][i * 128:(i + 1) * 128, :], xt[i][:, :])

    if upto >= 4:
        ffn(False)
    for i in range(NDC):
        S_.dma("sync", out_d[i * 128:(i + 1) * 128, :], xt[i][:, :])

    final = dict(S_.cnt)
    S_.prog["sync"].append(lambda e, v=final["lds"], h=S_.semh["lds"]: e.wait_ge(h, v))
    S_.prog["scalar"].append(lambda e, v=final["lda"], h=S_.semh["lda"]: e.wait_ge(h, v))
    S_.prog["gpsimd"].append(lambda e, v=final["ldg"], h=S_.semh["ldg"]: e.wait_ge(h, v))

    with nc.Block() as block:
        @block.sync
        def _(eng):
            for fn in S_.prog["sync"]:
                fn(eng)

        @block.scalar
        def _(eng):
            for fn in S_.prog["scalar"]:
                fn(eng)

        @block.tensor
        def _(eng):
            for fn in S_.prog["tensor"]:
                fn(eng)

        @block.vector
        def _(eng):
            for fn in S_.prog["vector"]:
                fn(eng)

        @block.gpsimd
        def _(eng):
            for fn in S_.prog["gpsimd"]:
                fn(eng)

    big.close()
    return nc


def _strips(w):
    O, K = w.shape
    onc = (O + 127) // 128
    if O % 128:
        w = np.pad(w, ((0, onc * 128 - O), (0, 0)))
    a = w.reshape(onc, 128, K // 128, 128)
    return np.ascontiguousarray(a.transpose(0, 3, 2, 1)).reshape(onc, 128, K)


def _pad_heads(w, H=16, dh=64):
    K = w.shape[1]
    out = np.zeros((H * 128, K), w.dtype)
    out.reshape(H, 128, K)[:, :dh] = w.reshape(H, dh, K)
    return out


def make_in_maps(L0_attn_a_x, L0_attn_a_Wq, L0_attn_a_Wk, L0_attn_a_Wv,
                 L0_attn_a_Wo, L0_attn_b_Wq, L0_attn_b_Wk, L0_attn_b_Wkg1,
                 L0_attn_b_Wkg2, L0_attn_b_bkg2, L0_attn_b_Wv, L0_attn_b_Wg,
                 L0_attn_b_bg, L0_attn_b_Wo, L0_ffn_w_gate, L0_ffn_w_up,
                 L0_ffn_w_down):
    f32 = np.float32
    shared = {}
    shared["wqa"] = _strips(np.asarray(L0_attn_a_Wq, f32))
    shared["wka"] = _strips(np.asarray(L0_attn_a_Wk, f32))
    shared["wva"] = _strips(np.asarray(L0_attn_a_Wv, f32))
    shared["woa"] = _strips(np.asarray(L0_attn_a_Wo, f32))
    shared["wqb"] = _strips(_pad_heads(np.asarray(L0_attn_b_Wq, f32)))
    shared["wkb"] = _strips(_pad_heads(np.asarray(L0_attn_b_Wk, f32)))
    shared["wvb"] = _strips(np.asarray(L0_attn_b_Wv, f32))
    shared["wgb"] = _strips(np.asarray(L0_attn_b_Wg, f32))
    shared["wob"] = _strips(np.asarray(L0_attn_b_Wo, f32))
    wkg1 = np.asarray(L0_attn_b_Wkg1, f32)
    shared["wkg1t"] = np.ascontiguousarray(
        wkg1.T.reshape(16, 128, 16).transpose(1, 0, 2).reshape(128, 256))
    wkg2 = np.asarray(L0_attn_b_Wkg2, f32)
    wkg2p = np.zeros((16, 2048), f32)
    wkg2p.reshape(16, 16, 128)[:, :, :64] = wkg2.reshape(16, 64, 16).transpose(2, 0, 1)
    shared["wkg2p"] = wkg2p
    bkg2p = np.zeros((128, 16), f32)
    bkg2p[:64, :] = np.asarray(L0_attn_b_bkg2, f32).reshape(16, 64).T
    shared["bkg2p"] = bkg2p
    shared["bgc"] = np.ascontiguousarray(
        np.asarray(L0_attn_b_bg, f32).reshape(16, 128).T)
    wg = np.asarray(L0_ffn_w_gate, f32)
    wu = np.asarray(L0_ffn_w_up, f32)
    wd = np.asarray(L0_ffn_w_down, f32)
    shared["wgf"] = _strips(wg)
    shared["wuf"] = _strips(wu)
    wdp = np.pad(wd, ((0, 0), (0, I_PAD - I_RAW)))
    aD = wdp.reshape(16, 128, NIC, 128)
    shared["wdf"] = np.ascontiguousarray(aD.transpose(0, 3, 2, 1)).reshape(
        16, 128, NIC * 128)
    shared["ident"] = np.eye(128, dtype=f32).astype(BF)
    shared["masku"] = np.triu(np.ones((128, 128), f32)).astype(BF)
    protm = np.zeros((128, 128), f32)
    for m in range(64):
        protm[m + 64, m] = -1.0
    for m in range(64, 128):
        protm[m - 64, m] = 1.0
    shared["prot"] = protm.astype(BF)
    shared["onecol"] = np.ones((128, 1), f32).astype(BF)
    shared["onerow"] = np.ones((1, 128), f32).astype(BF)

    HD = 128
    inv = 1.0 / 10000.0 ** (np.arange(0, HD, 2, dtype=np.float64) / HD)
    tpos = np.arange(S, dtype=np.float64)
    fr = np.outer(tpos, inv)
    emb = np.concatenate([fr, fr], -1)
    cosf = np.cos(emb).astype(f32)
    sinf = np.sin(emb).astype(f32)

    wsl3 = np.concatenate([wg.reshape(8, -1), wu.reshape(8, -1),
                           wd.reshape(8, -1)], axis=1)

    x = np.asarray(L0_attn_a_x, f32)
    in_maps = []
    for c in range(NCORE):
        b, hf = c // 2, c % 2
        m = dict(shared)
        m["x"] = np.ascontiguousarray(x[b, hf * T:(hf + 1) * T, :].T)
        m["cosr"] = np.ascontiguousarray(cosf[hf * T:(hf + 1) * T, :].T)
        m["sinr"] = np.ascontiguousarray(sinf[hf * T:(hf + 1) * T, :].T)
        sel = np.zeros((128, 8), f32)
        if hf == 1:
            sel[:, c - 1] = 1.0
        m["selw"] = sel
        m["wsl"] = np.ascontiguousarray(wsl3[c].reshape(128, 3 * 12928))
        in_maps.append(m)
    return in_maps


def kernel(**inputs):
    in_maps = make_in_maps(**inputs)
    nc = build_program(False)
    res = run_bass_kernel_spmd(nc, in_maps, core_ids=list(range(NCORE)))
    out = np.empty((B, S, D), np.float32)
    for c in range(NCORE):
        b, hf = c // 2, c % 2
        out[b, hf * T:(hf + 1) * T, :] = np.asarray(res.results[c]["out"]).T
    return out


# version 13 (final)
# speedup vs baseline: 1.0155x; 1.0155x over previous
import numpy as np
import ml_dtypes
from contextlib import ExitStack

import concourse.bass as bass
import concourse.mybir as mybir
from concourse.bass_utils import run_bass_kernel_spmd

F32 = mybir.dt.float32
BF16 = mybir.dt.bfloat16
I32 = mybir.dt.int32
AO = mybir.AluOpType
AF = mybir.ActivationFunctionType
AXX = mybir.AxisListType.X

NCORE = 8
B, S, D = 4, 2048, 2048
T = 1024
NDC = D // 128
CH = 128
NCH = T // CH
TW = 512
I_RAW = 6464
I_PAD = 6528
NIC = I_PAD // 128
FFN_CNT = float(I_RAW * D)
BF = ml_dtypes.bfloat16



def _A(x):
    if x is None or isinstance(x, (int, float)):
        return x
    if hasattr(x, "tensor"):
        return x
    try:
        return x[:]
    except Exception:
        return x

class Sched:
    STREAMS = ("sync", "scalar", "tensor", "vector", "gpsimd")

    def __init__(self, nc):
        self.nc = nc
        self.prog = {s: [] for s in self.STREAMS}
        self.semh = {}
        self.cnt = {}
        for name in ("lds", "lda", "ldg", "ve", "se", "te", "gp", "cc"):
            self.semh[name] = nc.alloc_semaphore(name=name)
            self.cnt[name] = 0
        self.wm = {s: {k: 0 for k in self.semh} for s in self.STREAMS}
        self.acc = {}
        self.retired = []
        self.born = {}

    def _names(self, aps):
        out = []
        for a in aps:
            if a is None or isinstance(a, (int, float)):
                continue
            t = getattr(a, "tensor", None)
            if t is None:
                t = a
            out.append(t.name)
        return out

    def _rec(self, name):
        if name not in self.acc:
            self.acc[name] = {"w": None, "r": []}
            self.born[name] = list(self.retired)
        return self.acc[name]

    def _deps(self, reads, writes):
        evs = []
        for n in reads:
            r = self._rec(n)
            if r["w"]:
                evs.append(r["w"])
            if self.born.get(n):
                evs.extend(self.born[n]); self.born[n] = []
        for n in writes:
            r = self._rec(n)
            if r["w"]:
                evs.append(r["w"])
            evs.extend(r["r"])
            if self.born.get(n):
                evs.extend(self.born[n]); self.born[n] = []
        ded = {}
        for s, v in evs:
            ded[s] = max(ded.get(s, 0), v)
        return ded

    def _commit(self, reads, writes, ev):
        for n in reads:
            self._rec(n)["r"].append(ev)
        for n in writes:
            r = self._rec(n)
            r["w"] = ev
            r["r"] = []

    def free(self, *handles):
        for h in handles:
            n = h.name if hasattr(h, "name") else h
            r = self.acc.get(n)
            if r:
                if r["w"]:
                    self.retired.append(r["w"])
                self.retired.extend(r["r"])
        ded = {}
        for s, v in self.retired:
            ded[s] = max(ded.get(s, 0), v)
        self.retired = [(s, v) for s, v in ded.items()]

    def _emit(self, stream, sem, inc, build, reads_aps, writes_aps, extra_ev=None):
        reads = self._names(reads_aps)
        writes = self._names(writes_aps)
        deps = self._deps(reads, writes)
        if extra_ev:
            for s, v in extra_ev:
                deps[s] = max(deps.get(s, 0), v)
        waits = []
        for s, v in deps.items():
            if self.wm[stream][s] < v:
                waits.append((self.semh[s], v))
                self.wm[stream][s] = v
        self.cnt[sem] += inc
        ev = (sem, self.cnt[sem])
        semh = self.semh[sem]

        def fn(eng, waits=waits, build=build, semh=semh, inc=inc):
            for h, v in waits:
                eng.wait_ge(h, v)
            ins = build(eng)
            if ins is not None:
                ins.then_inc(semh, inc)

        self.prog[stream].append(fn)
        self._commit(reads, writes, ev)
        return ev

    def dma(self, stream, out, in_, extra_ev=None, slow_ok=False):
        sem = {"sync": "lds", "scalar": "lda", "gpsimd": "ldg"}[stream]
        if slow_ok:
            return self._emit(stream, sem, 16,
                              lambda e: e.dma_start(out=out, in_=in_,
                                                    allow_slow_non_contiguous=True),
                              [in_], [out], extra_ev=extra_ev)
        return self._emit(stream, sem, 16,
                          lambda e: e.dma_start(out=out, in_=in_), [in_], [out],
                          extra_ev=extra_ev)

    def copy(self, out, in_, eng="vector"):
        out, in_ = _A(out), _A(in_)
        if eng == "scalar":
            return self._emit(eng, "se", 1,
                              lambda e: e.activation(out, in_, AF.Copy), [in_], [out])
        sem = {"vector": "ve", "gpsimd": "gp"}[eng]
        return self._emit(eng, sem, 1, lambda e: e.tensor_copy(out, in_), [in_], [out])

    def tt(self, out, a, b, op=AO.mult, eng="vector"):
        out, a, b = _A(out), _A(a), _A(b)
        sem = {"vector": "ve", "gpsimd": "gp"}[eng]
        return self._emit(eng, sem, 1, lambda e: e.tensor_tensor(out, a, b, op),
                          [a, b], [out])

    def ts(self, out, a, s1, s2=None, op0=AO.mult, op1=AO.bypass, eng="vector"):
        out, a, s1, s2 = _A(out), _A(a), _A(s1), _A(s2)
        sem = {"vector": "ve", "gpsimd": "gp"}[eng]
        rd = [a] + [x for x in (s1, s2) if hasattr(x, "tensor")]
        if s2 is None:
            return self._emit(eng, sem, 1,
                              lambda e: e.tensor_single_scalar(out, a, s1, op0),
                              rd, [out])
        return self._emit(eng, sem, 1,
                          lambda e: e.tensor_scalar(out, a, s1, s2, op0, op1),
                          rd, [out])

    def stt(self, out, in0, scalar, in1, op0, op1, eng="vector"):
        out, in0, scalar, in1 = _A(out), _A(in0), _A(scalar), _A(in1)
        sem = {"vector": "ve", "gpsimd": "gp"}[eng]
        rd = [in0, in1] + ([scalar] if hasattr(scalar, "tensor") else [])
        return self._emit(eng, sem, 1,
                          lambda e: e.scalar_tensor_tensor(out, in0, scalar, in1, op0, op1),
                          rd, [out])

    def scan(self, out, d0, d1, initial, op0, op1):
        out, d0, d1 = _A(out), _A(d0), _A(d1)
        rd = [d0, d1] + ([initial] if hasattr(initial, "tensor") else [])
        return self._emit("vector", "ve", 1,
                          lambda e: e.tensor_tensor_scan(out, d0, d1, initial, op0, op1),
                          rd, [out])

    def red(self, out, in_, op=AO.add, absval=False, negate=False):
        out, in_ = _A(out), _A(in_)
        def b(e):
            return e.tensor_reduce(out, in_, axis=AXX, op=op,
                                   apply_absolute_value=absval, negate=negate)
        return self._emit("vector", "ve", 1, b, [in_], [out])

    def recip(self, out, in_):
        out, in_ = _A(out), _A(in_)
        return self._emit("vector", "ve", 1, lambda e: e.reciprocal(out, in_),
                          [in_], [out])

    def act(self, out, in_, func, bias=0.0, scale=1.0):
        out, in_, bias, scale = _A(out), _A(in_), _A(bias), _A(scale)
        rd = [in_] + [x for x in (bias, scale) if hasattr(x, "tensor")]
        return self._emit("scalar", "se", 1,
                          lambda e: e.activation(out, in_, func, bias=bias, scale=scale),
                          rd, [out])

    def memset(self, ap, val, eng="vector"):
        ap = _A(ap)
        sem = {"vector": "ve", "gpsimd": "gp"}[eng]
        return self._emit(eng, sem, 1, lambda e: e.memset(ap, val), [], [ap])

    def mm(self, out, lhsT, rhs, start, stop, inc=True):
        out, lhsT, rhs = _A(out), _A(lhsT), _A(rhs)
        if not hasattr(self, "_g"):
            self._g = None
        def b(e, out=out, lhsT=lhsT, rhs=rhs, start=start, stop=stop):
            return e.matmul(out, lhsT, rhs, start=start, stop=stop)
        if self._g is None and start and stop:
            return self._emit("tensor", "te", 1, b, [lhsT, rhs], [out])
        if self._g is None:
            self._g = []
        self._g.append((b, [lhsT, rhs], [out]))
        if not stop:
            return None
        group = self._g
        self._g = None
        deps = {}
        for _, rd, wr in group:
            d = self._deps(self._names(rd), self._names(wr))
            for s, v in d.items():
                deps[s] = max(deps.get(s, 0), v)
        waits = []
        for s, v in deps.items():
            if self.wm["tensor"][s] < v:
                waits.append((self.semh[s], v))
                self.wm["tensor"][s] = v
        self.cnt["te"] += 1
        ev = ("te", self.cnt["te"])
        semh = self.semh["te"]

        def fn(eng, waits=waits, group=group, semh=semh):
            for h, v in waits:
                eng.wait_ge(h, v)
            ins = None
            for gb, _, _ in group:
                ins = gb(eng)
            ins.then_inc(semh, 1)

        self.prog["tensor"].append(fn)
        for _, rd, wr in group:
            self._commit(self._names(rd), self._names(wr), ev)
        return ev

    def transpose(self, out_psum, in_sb, ident):
        out_psum, in_sb, ident = _A(out_psum), _A(in_sb), _A(ident)
        kp = in_sb.shape[0]
        def b(e):
            return e.matmul(out_psum, in_sb, ident[0:kp, 0:kp], start=True, stop=True,
                            is_transpose=True)
        return self._emit("tensor", "te", 1, b, [in_sb, ident], [out_psum])

    def collective(self, kind, op, rg, ins, outs):
        return self._emit("gpsimd", "cc", 1,
                          lambda e: e.collective_compute(kind, op, rg, ins, outs),
                          ins, outs)


def build_program(debug=False, upto=99, asub=9):
    nc = bass.Bass("TRN2", target_bir_lowering=False, debug=False)
    S_ = Sched(nc)
    rg = [list(range(NCORE))]

    din = {}
    def IN(name, shape, dt=F32):
        din[name] = nc.dram_tensor(name, shape, dt, kind="ExternalInput").ap()
        return din[name]

    x_in = IN("x", [D, T])
    wqa_in = IN("wqa", [16, 128, 2048])
    wka_in = IN("wka", [4, 128, 2048])
    wva_in = IN("wva", [4, 128, 2048])
    woa_in = IN("woa", [16, 128, 2048])
    wqb_in = IN("wqb", [16, 128, 2048])
    wkb_in = IN("wkb", [16, 128, 2048])
    wvb_in = IN("wvb", [16, 128, 2048])
    wgb_in = IN("wgb", [16, 128, 2048])
    wob_in = IN("wob", [16, 128, 2048])
    wkg1_in = IN("wkg1t", [128, 256])
    wkg2_in = IN("wkg2p", [16, 2048])
    bkg2_in = IN("bkg2p", [128, 16])
    bg_in = IN("bgc", [128, 16])
    wg_in = IN("wgf", [NIC, 128, 2048])
    wu_in = IN("wuf", [NIC, 128, 2048])
    wd_in = IN("wdf", [16, 128, NIC * 128])
    wsl_in = IN("wsl", [128, 3 * 12928])
    ident_in = IN("ident", [128, 128], BF16)
    masku_in = IN("masku", [128, 128], BF16)
    prot_in = IN("prot", [128, 128], BF16)
    onecol_in = IN("onecol", [128, 1], BF16)
    onerow_in = IN("onerow", [1, 128], BF16)
    cos_in = IN("cosr", [128, T])
    sin_in = IN("sinr", [128, T])
    selw_in = IN("selw", [128, 8])
    out_d = nc.dram_tensor("out", [D, T], F32, kind="ExternalOutput").ap()
    dbg = {}
    if debug:
        for nm in ("d_ln1", "d_xa", "d_xf1", "d_xb"):
            dbg[nm] = nc.dram_tensor(nm, [D, T], F32, kind="ExternalOutput").ap()

    ar_in = nc.dram_tensor("ar_in", [1, 3], F32, kind="Internal").ap()
    ar_out = nc.dram_tensor("ar_out", [1, 3], F32, kind="Internal",
                            addr_space="Shared").ap()
    acc_d = nc.dram_tensor("acc_d", [128, 3], F32, kind="Internal").ap()
    scales_d = nc.dram_tensor("scales_d", [1, 9], F32, kind="Internal").ap()
    qdt = [nc.dram_tensor(f"qdt{i}", [1, 4, 128], F32, kind="Internal").ap()
           for i in range(3)]
    aga_in = nc.dram_tensor("aga_in", [128, 528], BF16, kind="Internal").ap()
    aga_out = nc.dram_tensor("aga_out", [NCORE, 128, 528], BF16, kind="Internal",
                             addr_space="Shared").ap()
    agb_in = nc.dram_tensor("agb_in", [64, 2048], BF16, kind="Internal").ap()
    agb_out = nc.dram_tensor("agb_out", [NCORE, 64, 2048], BF16, kind="Internal",
                             addr_space="Shared").ap()
    terng_d = nc.dram_tensor("terng_d", [NIC, 128, 2048], BF16, kind="Internal").ap()
    ternu_d = nc.dram_tensor("ternu_d", [NIC, 128, 2048], BF16, kind="Internal").ap()
    ternd_d = nc.dram_tensor("ternd_d", [16, 128, NIC * 128], BF16, kind="Internal").ap()
    kh_d = nc.dram_tensor("kh_d", [16, 128, T], BF16, kind="Internal").ap()
    gk_d = nc.dram_tensor("gk_d", [16, 128, T], BF16, kind="Internal").ap()
    vt_d = nc.dram_tensor("vt_d", [16, 128, T], BF16, kind="Internal").ap()

    big = ExitStack()
    def SB(name, shape, dt):
        return big.enter_context(nc.sbuf_tensor(name, shape, dt))

    xt = [SB(f"xt{i}", [128, T], F32) for i in range(NDC)]
    ln = [SB(f"ln{i}", [128, T], BF16) for i in range(NDC)]
    ident = SB("identS", [128, 128], BF16)
    masku = SB("maskuS", [128, 128], BF16)
    prot = SB("protS", [128, 128], BF16)
    onecol = SB("onecolS", [128, 1], BF16)
    onerow = SB("onerowS", [1, 128], BF16)
    cosb = SB("cosb", [128, T], BF16)
    sinb = SB("sinb", [128, T], BF16)
    selw = SB("selwS", [128, 8], F32)
    ccol = SB("ccol", [128, 8], F32)
    scol = SB("scol", [128, 12], F32)
    bkg2c = SB("bkg2cS", [128, 16], F32)
    bgc = SB("bgcS", [128, 16], F32)
    rows = SB("rows", [1, 2048], F32)
    rowb = SB("rowb", [1, 1024], BF16)
    ones32 = SB("ones32", [128, 128], F32)

    NHALF, PHALF, EPS5, EPS6, NEG1 = (ccol[:, i:i + 1] for i in range(5))

    pb = [big.enter_context(nc.psum_tensor(f"pb{i}", [128, 512], F32))
          for i in range(6)]
    pbA, pbB, pbC, pbD, pbE, pbF = pb
    ptA = big.enter_context(nc.psum_tensor("ptA", [128, 256], BF16))
    ptB = big.enter_context(nc.psum_tensor("ptB", [128, 256], BF16))

    S_.memset(ccol[:, 0:1], -0.5)
    S_.memset(ccol[:, 1:2], 0.5)
    S_.memset(ccol[:, 2:3], 1e-5)
    S_.memset(ccol[:, 3:4], 1e-6)
    S_.memset(ccol[:, 4:5], -1.0)
    S_.memset(ones32[:, :], 1.0)
    for i in range(NDC):
        S_.dma("sync", xt[i][:, :], x_in[i * 128:(i + 1) * 128, :])
    S_.dma("sync", ident[:, :], ident_in)
    S_.dma("sync", masku[:, :], masku_in)
    S_.dma("sync", prot[:, :], prot_in)
    S_.dma("sync", onecol[:, :], onecol_in)
    S_.dma("sync", onerow[:, :], onerow_in)
    S_.dma("sync", selw[:, :], selw_in)
    S_.dma("sync", bkg2c[:, :], bkg2_in)
    S_.dma("sync", bgc[:, :], bg_in)

    p0 = ExitStack()
    def P0(name, shape, dt):
        return p0.enter_context(nc.sbuf_tensor(name, shape, dt))
    cs32 = P0("cs32", [128, T], F32)
    S_.dma("sync", cs32[:, :], cos_in)
    S_.copy(cosb[:, :], cs32[:, :])
    S_.dma("sync", cs32[:, :], sin_in)
    S_.copy(sinb[:, :], cs32[:, :])

    wslb = [P0(f"wslb{i}", [128, 6464], F32) for i in range(2)]
    acc3 = P0("acc3", [128, 4], F32)
    rc1 = P0("rc1", [128, 1], F32)
    for mi in range(3):
        for hf in range(2):
            buf = wslb[hf]
            sl = slice(mi * 12928 + hf * 6464, mi * 12928 + (hf + 1) * 6464)
            S_.dma("scalar", buf[:, :], wsl_in[:, sl])
            if hf == 0:
                S_.red(acc3[:, mi:mi + 1], buf[:, :], AO.add, absval=True)
            else:
                S_.red(rc1[:, :], buf[:, :], AO.add, absval=True)
                S_.tt(acc3[:, mi:mi + 1], acc3[:, mi:mi + 1], rc1[:, :], AO.add)
    accrow = P0("accrow", [1, 384], F32)
    S_.dma("sync", acc_d, acc3[:, 0:3])
    S_.dma("sync", accrow[0:1, :], acc_d)
    for mi in range(3):
        S_.red(rows[0:1, 16 + mi:17 + mi], accrow[0:1, mi::3], AO.add)
    S_.dma("sync", ar_in, rows[0:1, 16:19])
    S_.collective("AllReduce", AO.add, rg, [ar_in], [ar_out])
    S_.dma("sync", rows[0:1, 0:3], ar_out)
    S_.ts(rows[0:1, 8:11], rows[0:1, 0:3], 1.0 / FFN_CNT, 1e-5, AO.mult, AO.max)
    S_.recip(rows[0:1, 11:14], rows[0:1, 8:11])
    S_.ts(rows[0:1, 14:17], rows[0:1, 8:11], 0.5, None, AO.mult)
    S_.dma("sync", scales_d, rows[0:1, 8:17])
    S_.dma("sync", scol[:, 0:9], scales_d.broadcast_to([128, 9]))
    p0.close()
    S_.free("cs32", "wslb0", "wslb1", "acc3", "rc1", "accrow")

    lnscr = [SB(f"lnscr{i}", [128, 512], BF16) for i in range(2)]
    ln_n = [0]

    def layernorm():
        ln_n[0] += 1
        les = ExitStack()
        sqt = [les.enter_context(nc.sbuf_tensor(f"lnsq{ln_n[0]}_{i}", [128, 512], BF16))
               for i in range(NDC)]
        for tt in range(2):
            tsl = slice(tt * TW, (tt + 1) * TW)
            for i in range(NDC):
                S_.copy(ln[i][:, tsl], xt[i][:, tsl],
                        eng=("vector" if i % 2 == 0 else "vector"))
            for i in range(NDC):
                S_.mm(pbC[0:1, 0:512], onecol, ln[i][:, tsl],
                      start=(i == 0), stop=(i == NDC - 1), inc=(i == NDC - 1))
            for i in range(NDC):
                S_.act(sqt[i][:, :], ln[i][:, tsl], AF.Square)
            for i in range(NDC):
                S_.mm(pbD[0:1, 0:512], onecol, sqt[i][:, :],
                      start=(i == 0), stop=(i == NDC - 1), inc=(i == NDC - 1))
            mu = rows[0:1, 0:512]
            musq = rows[0:1, 512:1024]
            var = rows[0:1, 1024:1536]
            rstd = rows[0:1, 1536:2048]
            S_.ts(mu, pbC[0:1, 0:512], 1.0 / D, None, AO.mult)
            S_.tt(musq, mu, mu, AO.mult)
            S_.stt(var, pbD[0:1, 0:512], 1.0 / D, musq, AO.mult, AO.subtract)
            S_.act(var, var, AF.Sqrt, bias=ccol[0:1, 2:3])
            S_.recip(rstd, var)
            S_.copy(rowb[0:1, 0:512], rstd)
            S_.stt(musq, mu, -1.0, rstd, AO.mult, AO.mult)
            S_.copy(rowb[0:1, 512:1024], musq)
            S_.mm(pbC[:, 0:512], onerow, rowb[0:1, 0:512], start=True, stop=True)
            S_.mm(pbD[:, 0:512], onerow, rowb[0:1, 512:1024], start=True, stop=True)
            S_.ts(lnscr[0][:, :], pbC[0:128, 0:512], 1.0, None, AO.mult)
            S_.ts(lnscr[1][:, :], pbD[0:128, 0:512], 1.0, None, AO.mult)
            for i in range(NDC):
                S_.tt(ln[i][:, tsl], ln[i][:, tsl], lnscr[0][:, :], AO.mult)
                S_.tt(ln[i][:, tsl], ln[i][:, tsl], lnscr[1][:, :], AO.add)
        les.close()
        S_.free(*[f"lnsq{ln_n[0]}_{i}" for i in range(NDC)])

    def project(strips_ap, n_oc, consumer, bufs, bank_of=None, rhs=None, tts=(0, 1)):
        rhs = rhs or ln
        for oc in range(n_oc):
            buf = bufs[oc % len(bufs)]
            S_.dma("gpsimd", buf[:, :], strips_ap[oc])
            for tt in tts:
                tsl = slice(tt * TW, (tt + 1) * TW)
                bank = (bank_of((oc * len(tts) + (tt - tts[0])) % 2) if bank_of
                        else pb[(oc * len(tts) + tt) % 2])
                for dc in range(NDC):
                    S_.mm(bank[:, 0:512], buf[:, dc * 128:(dc + 1) * 128],
                          rhs[dc][:, tsl], start=(dc == 0), stop=(dc == NDC - 1),
                          inc=(dc == NDC - 1))
                consumer(oc, tt, bank)

    if upto >= 1:
        layernorm()

    if upto >= 1:
        aa = ExitStack()
        def AA(name, shape, dt):
            return aa.enter_context(nc.sbuf_tensor(name, shape, dt))

        qf = [AA(f"qf{h}", [128, T], BF16) for h in range(16)]
        kf = [AA(f"kf{g}", [128, T], BF16) for g in range(4)]
        vtm = [AA(f"vtm{g}", [128, NCH * 132], BF16) for g in range(4)]
        srun = AA("srun", [128, 4 * 132], F32)
        sbfa = AA("sbfa", [128, 4 * 132], BF16)
        rsc = [AA(f"rsc{i}", [128, 512], BF16) for i in range(6)]
        ktm = AA("ktm", [128, 128], BF16)
        ktm2 = AA("ktm2", [128, 128], BF16)
        rowsB = AA("rowsB", [1, 1024], F32)
        rowbB = AA("rowbB", [1, 512], BF16)
        strips = [AA(f"strip{i}", [128, 2048], BF16) for i in range(3)]

        def rope_elu(dst, oc, tt, bank):
            tsl = slice(tt * TW, (tt + 1) * TW)
            S_.act(rsc[0][:, :], bank[:, 0:512], AF.Copy)
            S_.mm(pbC[:, 0:512], prot, rsc[0][:, :], start=True, stop=True)
            S_.tt(rsc[1][:, :], rsc[0][:, :], cosb[:, tsl], AO.mult)
            S_.stt(rsc[2][:, :], pbC[0:128, 0:512], 1.0, sinb[:, tsl], AO.mult, AO.mult)
            S_.tt(rsc[2][:, :], rsc[2][:, :], rsc[1][:, :], AO.add)
            S_.ts(rsc[3][:, :], rsc[2][:, :], 0.0, None, AO.min)
            S_.act(rsc[4][:, :], rsc[3][:, :], AF.Exp)
            S_.stt(dst[oc][:, tsl], rsc[2][:, :], 0.0, rsc[4][:, :], AO.max, AO.add)

        if asub >= 1:
            project(wka_in, 4, lambda oc, tt, bank: rope_elu(kf, oc, tt, bank), strips)

        for g in range(4):
            S_.memset(vtm[g][:, :], 1.0)

        def v_consume(oc, tt, bank):
            S_.act(rsc[5][:, :], bank[:, 0:512], AF.Copy)
            for j in range(4):
                c = tt * 4 + j
                ptj = ptA if j % 2 == 0 else ptB
                S_.transpose(ptj[:, 0:128], rsc[5][:, j * 128:(j + 1) * 128], ident)
                S_.copy(vtm[oc][:, c * 132:c * 132 + 128], ptj[0:128, 0:128], eng="scalar")
        if asub >= 1:
            project(wva_in, 4, v_consume, strips)
            project(wqa_in, 16, lambda oc, tt, bank: rope_elu(qf, oc, tt, bank), strips)

        def a_state_mm(g, c):
            csl = slice(c * CH, (c + 1) * CH)
            ptj = ptA if g % 2 == 0 else ptB
            ktmj = ktm if g % 2 == 0 else ktm2
            S_.transpose(ptj[:, 128:256], kf[g][:, csl], ident)
            S_.copy(ktmj[:, :], ptj[0:128, 128:256], eng="scalar")
            S_.mm(pbF[:, 0:129], ktmj, vtm[g][:, c * 132:c * 132 + 129],
                  start=True, stop=True)

        if asub >= 2:
            for c in range(NCH):
                for g in range(4):
                    a_state_mm(g, c)
                    ssl = slice(g * 132, g * 132 + 129)
                    if c == 0:
                        S_.ts(srun[:, ssl], pbF[0:128, 0:129], 1.0, None, AO.mult)
                    else:
                        S_.tt(srun[:, ssl], srun[:, ssl], pbF[0:128, 0:129], AO.add)
            S_.copy(sbfa[:, :], srun[:, :])
        else:
            S_.memset(srun[:, :], 0.0)
            S_.copy(sbfa[:, :], srun[:, :])
        if asub >= 3:
            S_.dma("sync", aga_in, sbfa[:, 0:528])
            S_.collective("AllGather", AO.bypass, rg, [aga_in], [aga_out])

        agr = [AA(f"agr{i}", [128, 528], BF16) for i in range(2)]
        for r in (range(NCORE) if asub >= 3 else []):
            buf = agr[r % 2]
            S_.dma("sync", buf[:, :], aga_out[r])
            if r == 0:
                S_.ts(srun[:, :], buf[:, :], selw[:, 0:1], None, AO.mult)
            else:
                S_.stt(srun[:, :], buf[:, :], selw[:, r:r + 1], srun[:, :],
                       AO.mult, AO.add)
        S_.copy(sbfa[:, :], srun[:, :])

        for c in range(NCH):
            csl = slice(c * CH, (c + 1) * CH)
            for g in range(4):
                hs = [4 * g + i for i in range(4)]
                bA_ = pbD if g % 2 == 0 else pbC
                bN_ = pbE if g % 2 == 0 else pbF
                bDen = pbA if g % 2 == 0 else pbB
                rws = rows if g % 2 == 0 else rowsB
                rwb = rowb if g % 2 == 0 else rowbB
                rbc = rsc[4] if g % 2 == 0 else rsc[5]
                for i, h in enumerate(hs):
                    S_.mm(bA_[:, i * 128:(i + 1) * 128], kf[g][:, csl], qf[h][:, csl],
                          start=(i == 0), stop=(i == 3), inc=(i == 3))
                for i, h in enumerate(hs):
                    S_.tt(rsc[i][:, 0:128], bA_[0:128, i * 128:(i + 1) * 128],
                          masku, AO.mult)
                for i, h in enumerate(hs):
                    S_.mm(bN_[:, i * 128:(i + 1) * 128], vtm[g][:, c * 132:c * 132 + 128],
                          rsc[i][:, 0:128], start=(i == 0), stop=False, inc=False)
                    S_.mm(bN_[:, i * 128:(i + 1) * 128], sbfa[:, g * 132:g * 132 + 128],
                          qf[h][:, csl], start=False, stop=(i == 3), inc=(i == 3))
                for i, h in enumerate(hs):
                    S_.mm(bDen[0:1, i * 128:(i + 1) * 128], onecol, rsc[i][:, 0:128],
                          start=(i == 0), stop=False, inc=False)
                    S_.mm(bDen[0:1, i * 128:(i + 1) * 128],
                          sbfa[:, g * 132 + 128:g * 132 + 129], qf[h][:, csl],
                          start=False, stop=(i == 3), inc=(i == 3))
                S_.ts(rws[0:1, 0:512], bDen[0:1, 0:512], 1e-6, None, AO.add)
                S_.recip(rws[0:1, 512:1024], rws[0:1, 0:512])
                S_.copy(rwb[0:1, 0:512], rws[0:1, 512:1024])
                S_.mm(bA_[:, 0:512], onerow, rwb[0:1, 0:512], start=True, stop=True)
                S_.ts(rbc[:, :], bA_[0:128, 0:512], 1.0, None, AO.mult)
                for i, h in enumerate(hs):
                    S_.tt(qf[h][:, csl], bN_[0:128, i * 128:(i + 1) * 128],
                          rbc[:, i * 128:(i + 1) * 128], AO.mult)
            if c < NCH - 1:
                for g in range(4):
                    a_state_mm(g, c)
                    ssl = slice(g * 132, g * 132 + 129)
                    S_.tt(srun[:, ssl], srun[:, ssl], pbF[0:128, 0:129], AO.add)
                    S_.copy(sbfa[:, ssl], srun[:, ssl])

        def woa_consume(oc, tt, bank):
            tsl = slice(tt * TW, (tt + 1) * TW)
            S_.stt(xt[oc][:, tsl], bank[:, 0:512], 1.0, xt[oc][:, tsl], AO.mult, AO.add)
        if asub >= 5:
            project(woa_in, 16, woa_consume, strips, rhs=qf)

        aa.close()
        S_.free(*[f"qf{h}" for h in range(16)], *[f"kf{g}" for g in range(4)],
                *[f"vtm{g}" for g in range(4)], "srun", "sbfa",
                *[f"rsc{i}" for i in range(6)], "ktm", "strip0", "strip1",
                "agr0", "agr1")
        if debug:
            for i in range(NDC):
                S_.dma("sync", dbg["d_xa"][i * 128:(i + 1) * 128, :], xt[i][:, :])

    def ffn(first):
        layernorm()
        fs = ExitStack()
        pfx = "fA_" if first else "fB_"
        def FF(name, shape, dt):
            return fs.enter_context(nc.sbuf_tensor(pfx + name, shape, dt))
        hbuf = [FF(f"h{ic}", [128, 512], BF16) for ic in range(NIC)]
        fsc = [FF(f"fsc{i}", [128, 512], BF16) for i in range(2)]
        hgt = FF("hgt", [128, 512], BF16)
        hut = FF("hut", [128, 512], BF16)
        sxb = FF("sxb", [128, 512], F32)
        fb = [FF(f"fb{i}", [128, 512], F32) for i in range(2)]
        q32 = FF("q32", [128, 512], F32)
        qi = FF("qi", [128, 512], I32)
        nb = 2 if first else 3
        if first:
            wf32 = [FF(f"wf32{i}", [128, 1024], F32) for i in range(2)]
            sg2 = [FF(f"sg2{i}", [128, 1024], BF16) for i in range(2)]
        terng = [FF(f"terng{i}", [128, 2048], BF16) for i in range(nb)]
        ternu = [FF(f"ternu{i}", [128, 2048], BF16) for i in range(nb)]

        def quant_rows(tiles, n, scis, outs):
            S_.tt(fsc[0][:, :], tiles[0], tiles[1], AO.max)
            S_.tt(fsc[1][:, :], tiles[0], tiles[1], AO.min)
            for i in range(2, n):
                S_.tt(fsc[0][:, :], fsc[0][:, :], tiles[i], AO.max)
                S_.tt(fsc[1][:, :], fsc[1][:, :], tiles[i], AO.min)
            S_.ts(fsc[1][:, :], fsc[1][:, :], -1.0, None, AO.mult)
            S_.tt(fsc[0][:, :], fsc[0][:, :], fsc[1][:, :], AO.max)
            for j in range(4):
                ptj = ptA if j % 2 == 0 else ptB
                S_.transpose(ptj[:, (j // 2) * 128:(j // 2) * 128 + 128],
                             fsc[0][:, j * 128:(j + 1) * 128], ident)
            for j in range(4):
                ptj = ptA if j % 2 == 0 else ptB
                S_.red(q32[:, j:j + 1], ptj[0:128, (j // 2) * 128:(j // 2) * 128 + 128],
                       AO.max)
            S_.ts(q32[:, 4:8], q32[:, 0:4], 1e-5, 1.0 / 127.0, AO.max, AO.mult)
            S_.recip(q32[:, 8:12], q32[:, 4:8])
            S_.dma("sync", qdt[0].rearrange("a j p -> p (a j)"), q32[:, 8:12], slow_ok=True)
            for i, sci in enumerate(scis):
                S_.ts(q32[:, 12 + 4 * i:16 + 4 * i], q32[:, 4:8],
                      scol[:, 6 + sci:7 + sci], None, AO.mult)
                S_.dma("sync", qdt[1 + i].rearrange("a j p -> p (a j)"),
                       q32[:, 12 + 4 * i:16 + 4 * i], slow_ok=True)
            for i, o in enumerate(outs):
                S_.dma("sync", o[:, :],
                       qdt[i].rearrange("a j p -> a (j p)").broadcast_to([128, 512]))

        def quantize(tile_ap):
            S_.tt(q32[:, :], tile_ap, sxb[:, :], AO.mult)
            S_.act(qi[:, :], q32[:, :], AF.Copy)
            S_.copy(tile_ap, qi[:, :])

        def tern_load(dst, src_ap, width, sci):
            nh = (width + 1023) // 1024
            for hf in range(nh):
                w = min(1024, width - hf * 1024)
                buf = wf32[hf % 2]
                S_.dma("scalar", buf[:, 0:w], src_ap[:, hf * 1024:hf * 1024 + w])
                S_.act(sg2[0][:, 0:w], buf[:, 0:w], AF.Sign, bias=NHALF,
                       scale=scol[:, 3 + sci:4 + sci])
                S_.act(sg2[1][:, 0:w], buf[:, 0:w], AF.Sign, bias=PHALF,
                       scale=scol[:, 3 + sci:4 + sci])
                S_.tt(dst[:, hf * 1024:hf * 1024 + w], sg2[0][:, 0:w],
                      sg2[1][:, 0:w], AO.add)

        for tt in range(2):
            tsl = slice(tt * TW, (tt + 1) * TW)
            lslices = [ln[i][:, tsl] for i in range(NDC)]
            quant_rows(lslices, NDC, (0, 1), (sxb, fb[0], fb[1]))
            for i in range(NDC):
                quantize(ln[i][:, tsl])
            for oc in range(NIC):
                tg, tu = terng[oc % nb], ternu[oc % nb]
                if first and tt == 0:
                    tern_load(tg, wg_in[oc], 2048, 0)
                    S_.dma("sync", terng_d[oc], tg[:, :])
                    tern_load(tu, wu_in[oc], 2048, 1)
                    S_.dma("sync", ternu_d[oc], tu[:, :])
                else:
                    S_.dma("scalar" if oc % 2 == 0 else "sync", tg[:, :], terng_d[oc])
                    S_.dma("scalar" if oc % 2 == 1 else "sync", tu[:, :], ternu_d[oc])
                bg_, bu_ = [(pbA, pbB), (pbC, pbD), (pbE, pbF)][oc % 3]
                for dc in range(NDC):
                    S_.mm(bg_[:, 0:512], tg[:, dc * 128:(dc + 1) * 128],
                          ln[dc][:, tsl], start=(dc == 0), stop=(dc == NDC - 1),
                          inc=(dc == NDC - 1))
                for dc in range(NDC):
                    S_.mm(bu_[:, 0:512], tu[:, dc * 128:(dc + 1) * 128],
                          ln[dc][:, tsl], start=(dc == 0), stop=(dc == NDC - 1),
                          inc=(dc == NDC - 1))
                S_.tt(q32[:, :], bg_[0:128, 0:512], fb[0][:, :], AO.mult)
                S_.act(hgt[:, :], q32[:, :], AF.Silu)
                S_.tt(hut[:, :], bu_[0:128, 0:512], fb[1][:, :], AO.mult)
                S_.tt(hbuf[oc][:, :], hgt[:, :], hut[:, :], AO.mult)
            hs = [hbuf[i][:, :] for i in range(NIC)]
            quant_rows(hs, NIC, (2,), (sxb, fb[0]))
            for ic in range(NIC):
                quantize(hbuf[ic][:, :])
            for oc in range(NDC):
                bd_ = pbE if oc % 2 == 0 else pbF
                for qtr in range(4):
                    k0 = qtr * 13
                    nk = 13 if qtr < 3 else NIC - 39
                    buf = [terng[0], ternu[0], terng[1], ternu[1]][qtr]
                    if first and tt == 0:
                        tern_load(buf[:, 0:nk * 128],
                                  wd_in[oc][:, k0 * 128:(k0 + nk) * 128],
                                  nk * 128, 2)
                        S_.dma("sync", ternd_d[oc][:, k0 * 128:(k0 + nk) * 128],
                               buf[:, 0:nk * 128])
                    else:
                        S_.dma("scalar", buf[:, 0:nk * 128],
                               ternd_d[oc][:, k0 * 128:(k0 + nk) * 128])
                    for k in range(nk):
                        kc = k0 + k
                        S_.mm(bd_[:, 0:512], buf[:, k * 128:(k + 1) * 128],
                              hbuf[kc][:, :], start=(kc == 0), stop=(kc == NIC - 1),
                              inc=(kc == NIC - 1))
                S_.tt(q32[:, :], bd_[0:128, 0:512], fb[0][:, :], AO.mult)
                S_.tt(xt[oc][:, tsl], xt[oc][:, tsl], q32[:, :], AO.add)
        fs.close()
        S_.free(*[pfx + f"h{ic}" for ic in range(NIC)],
                *[pfx + f"fsc{i}" for i in range(2)],
                pfx + "hgt", pfx + "hut", pfx + "sxb", pfx + "fb0", pfx + "fb1",
                pfx + "q32", pfx + "qi",
                *[pfx + f"terng{i}" for i in range(nb)],
                *[pfx + f"ternu{i}" for i in range(nb)],
                *([pfx + "sg20", pfx + "sg21", pfx + "wf320", pfx + "wf321"]
                  if first else []))

    if upto >= 2:
        ffn(True)
    if debug:
        for i in range(NDC):
            S_.dma("sync", dbg["d_xf1"][i * 128:(i + 1) * 128, :], xt[i][:, :])

    if upto >= 3:
        layernorm()
        H = 16

        b3 = ExitStack()
        def B3(name, shape, dt):
            return b3.enter_context(nc.sbuf_tensor(name, shape, dt))
        sgla = B3("sgla", [128, H * 128], F32)
        sglab = B3("sglab", [128, H * 128], BF16)
        egl = B3("egl", [128, H * NCH], F32)
        G32p = [B3(f"G32_{i}", [128, 128], F32) for i in range(2)]
        gchp = [B3(f"gch_{i}", [128, 128], F32) for i in range(2)]
        e32p = [[B3(f"e32{i}_{j}", [128, 128], BF16) for i in range(4)]
                for j in range(2)]
        mcolp = [B3(f"mcol_{i}", [128, 4], F32) for i in range(2)]
        btmp = [B3(f"btm_{i}", [128, 128], BF16) for i in range(2)]

        S_.memset(sgla[:, :], 0.0)

        for hh in range(2):
            hsl_t = slice(hh * 512, (hh + 1) * 512)
            b1 = ExitStack()
            def B1(name, shape, dt):
                return b1.enter_context(nc.sbuf_tensor(f"p1h{hh}_" + name, shape, dt))
            khb = [B1(f"khb{h}", [128, 512], BF16) for h in range(H)]
            gkb = [B1(f"gkb{h}", [128, 512], BF16) for h in range(H)]
            vtb = [B1(f"vtb{h}", [128, 512], BF16) for h in range(H)]
            bstr = [B1(f"bstr{i}", [128, 2048], BF16) for i in range(2)]
            kg1 = B1("kg1", [16, 512], BF16)
            wkg1s = B1("wkg1s", [128, 256], BF16)
            wkg2s = B1("wkg2s", [16, 2048], BF16)
            sg2v = B1("sg2v", [128, 512], BF16)

            def k_consume(oc, tt, bank):
                S_.ts(khb[oc][:, :], bank[:, 0:512], float(128.0 ** -0.5), None, AO.mult)
            project(wkb_in, 16, k_consume, bstr, tts=(hh,))

            def vb_consume(oc, tt, bank):
                S_.act(sg2v[:, :], bank[:, 0:512], AF.Copy)
                for j in range(4):
                    ptj = ptA if j % 2 == 0 else ptB
                    S_.transpose(ptj[:, 0:128], sg2v[:, j * 128:(j + 1) * 128], ident)
                    S_.copy(vtb[oc][:, j * 128:(j + 1) * 128], ptj[0:128, 0:128],
                            eng="scalar")
            project(wvb_in, 16, vb_consume, bstr, tts=(hh,))

            S_.dma("gpsimd", wkg1s[:, :], wkg1_in)
            S_.dma("gpsimd", wkg2s[:, :], wkg2_in)
            for dc in range(NDC):
                S_.mm(pbC[0:16, 0:512], wkg1s[:, dc * 16:(dc + 1) * 16],
                      ln[dc][:, hsl_t], start=(dc == 0), stop=(dc == NDC - 1),
                      inc=(dc == NDC - 1))
            S_.copy(kg1[:, :], pbC[0:16, 0:512])
            for h in range(H):
                S_.mm(pbD[:, 0:512], wkg2s[:, h * 128:(h + 1) * 128], kg1[:, :],
                      start=True, stop=True)
                S_.act(sg2v[:, :], pbD[0:128, 0:512], AF.Sigmoid, bias=bkg2c[:, h:h + 1])
                S_.act(gkb[h][:, :], sg2v[:, :], AF.Ln, scale=1.0)

            for cc in range(4):
                c = hh * 4 + cc
                csl = slice(cc * CH, (cc + 1) * CH)
                for h in range(H):
                    hsl = slice(h * 128, (h + 1) * 128)
                    G32, gch, e32 = G32p[h % 2], gchp[h % 2], e32p[h % 2]
                    mcol, btm = mcolp[h % 2], btmp[h % 2]
                    S_.ts(gch[:, :], gkb[h][:, csl], 1.0 / 16.0, None, AO.mult)
                    S_.scan(G32[:, :], ones32, gch, 0.0, AO.mult, AO.add)
                    S_.ts(mcol[:, 0:1], G32[:, 127:128], 1.0, None, AO.mult)
                    S_.act(egl[:, h * NCH + c:h * NCH + c + 1], mcol[:, 0:1], AF.Exp)
                    S_.act(e32[3][:, :], G32[:, :], AF.Exp, bias=mcol[:, 0:1], scale=NEG1)
                    S_.tt(e32[3][:, :], khb[h][:, csl], e32[3][:, :], AO.mult)
                    ptj = ptA if h % 2 == 0 else ptB
                    S_.transpose(ptj[:, 0:128], e32[3][:, :], ident)
                    S_.copy(btm[:, :], ptj[0:128, 0:128], eng="scalar")
                    bst = pbE if h % 2 == 0 else pbF
                    S_.mm(bst[:, 0:128], btm, vtb[h][:, csl], start=True, stop=True)
                    S_.ts(sgla[:, hsl], sgla[:, hsl],
                          egl[:, h * NCH + c:h * NCH + c + 1], None, AO.mult)
                    S_.tt(sgla[:, hsl], sgla[:, hsl], bst[0:128, 0:128], AO.add)

            for h in range(H):
                S_.dma("scalar", kh_d[h][:, hsl_t], khb[h][:, :])
                S_.dma("scalar", gk_d[h][:, hsl_t], gkb[h][:, :])
                S_.dma("scalar", vt_d[h][:, hsl_t], vtb[h][:, :])
            b1.close()
            S_.free(*[f"p1h{hh}_khb{h}" for h in range(H)],
                    *[f"p1h{hh}_gkb{h}" for h in range(H)],
                    *[f"p1h{hh}_vtb{h}" for h in range(H)],
                    f"p1h{hh}_bstr0", f"p1h{hh}_bstr1", f"p1h{hh}_kg1",
                    f"p1h{hh}_wkg1s", f"p1h{hh}_wkg2s", f"p1h{hh}_sg2v")

        S_.copy(sglab[:, :], sgla[:, :])
        S_.dma("sync", agb_in, sglab[0:64, :])
        S_.collective("AllGather", AO.bypass, rg, [agb_in], [agb_out])
        ba = ExitStack()
        agrb = [ba.enter_context(nc.sbuf_tensor(f"agrb{i}", [64, 2048], BF16))
                for i in range(2)]
        for r in range(NCORE):
            buf = agrb[r % 2]
            S_.dma("sync", buf[:, :], agb_out[r])
            if r == 0:
                S_.ts(sgla[0:64, :], buf[:, :], selw[0:64, 0:1], None, AO.mult)
            else:
                S_.stt(sgla[0:64, :], buf[:, :], selw[0:64, r:r + 1], sgla[0:64, :],
                       AO.mult, AO.add)
        S_.memset(sgla[64:128, :], 0.0)
        S_.copy(sglab[:, :], sgla[:, :])
        ba.close()
        S_.free("agrb0", "agrb1")

        for hh in range(2):
            hsl_t = slice(hh * 512, (hh + 1) * 512)
            b2 = ExitStack()
            def B2(name, shape, dt):
                return b2.enter_context(nc.sbuf_tensor(f"p2h{hh}_" + name, shape, dt))
            qh2 = [B2(f"qh2_{h}", [128, 512], BF16) for h in range(H)]
            kh2 = [B2(f"kh2_{h}", [128, 512], BF16) for h in range(H)]
            gk2 = [B2(f"gk2_{h}", [128, 512], BF16) for h in range(H)]
            vt2 = [B2(f"vt2_{h}", [128, 512], BF16) for h in range(H)]
            bstr2 = [B2(f"bstr2_{i}", [128, 2048], BF16) for i in range(2)]
            scr2 = [B2(f"scr2_{i}", [128, 128], BF16) for i in range(4)]
            gvt = B2("gvt", [128, 512], BF16)
            sq2 = B2("sq2", [128, 512], BF16)

            for h in range(H):
                S_.dma("scalar", kh2[h][:, :], kh_d[h][:, hsl_t])
                S_.dma("scalar", gk2[h][:, :], gk_d[h][:, hsl_t])
                S_.dma("scalar", vt2[h][:, :], vt_d[h][:, hsl_t])

            def q_consume(oc, tt, bank):
                S_.ts(qh2[oc][:, :], bank[:, 0:512], 1.0, None, AO.mult)
            project(wqb_in, 16, q_consume, bstr2, tts=(hh,))

            for cc in range(4):
                c = hh * 4 + cc
                csl = slice(cc * CH, (cc + 1) * CH)
                for h in range(H):
                    hsl = slice(h * 128, (h + 1) * 128)
                    G32, gch, e32 = G32p[h % 2], gchp[h % 2], e32p[h % 2]
                    mcol, btm = mcolp[h % 2], btmp[h % 2]
                    sc0, sc1 = scr2[h % 2 * 2], scr2[h % 2 * 2 + 1]
                    bA = pbC if h % 2 == 0 else pbD
                    S_.ts(gch[:, :], gk2[h][:, csl], 1.0 / 16.0, None, AO.mult)
                    S_.scan(G32[:, :], ones32, gch, 0.0, AO.mult, AO.add)
                    S_.ts(mcol[:, 0:1], G32[:, 127:128], 1.0, None, AO.mult)
                    S_.act(egl[:, h * NCH + c:h * NCH + c + 1], mcol[:, 0:1], AF.Exp)
                    S_.act(e32[3][:, :], G32[:, :], AF.Exp, bias=mcol[:, 0:1], scale=NEG1)
                    S_.ts(mcol[:, 1:2], G32[:, 127:128], 0.5, None, AO.mult)
                    S_.ts(mcol[:, 2:3], mcol[:, 1:2], -1.0, None, AO.mult)
                    S_.act(e32[0][:, :], G32[:, :], AF.Exp, bias=mcol[:, 2:3])
                    S_.act(e32[1][:, :], G32[:, :], AF.Exp, bias=mcol[:, 1:2], scale=NEG1)
                    S_.act(e32[2][:, :], G32[:, :], AF.Exp)
                    S_.tt(sc0[:, :], qh2[h][:, csl], e32[0][:, :], AO.mult)
                    S_.tt(sc1[:, :], kh2[h][:, csl], e32[1][:, :], AO.mult)
                    S_.mm(bA[:, 0:128], sc1, sc0, start=True, stop=True)
                    S_.tt(sc0[:, :], bA[0:128, 0:128], masku, AO.mult)
                    S_.tt(sc1[:, :], qh2[h][:, csl], e32[2][:, :], AO.mult)
                    S_.mm(bA[:, 128:256], vt2[h][:, csl], sc0, start=True,
                          stop=False, inc=False)
                    S_.mm(bA[:, 128:256], sglab[:, hsl], sc1, start=False,
                          stop=True, inc=True)
                    S_.copy(qh2[h][:, csl], bA[0:128, 128:256])
                    S_.tt(e32[3][:, :], kh2[h][:, csl], e32[3][:, :], AO.mult)
                    ptj = ptA if h % 2 == 0 else ptB
                    S_.transpose(ptj[:, 0:128], e32[3][:, :], ident)
                    S_.copy(btm[:, :], ptj[0:128, 0:128], eng="scalar")
                    bK = pbE if h % 2 == 0 else pbF
                    S_.mm(bK[:, 0:128], btm, vt2[h][:, csl], start=True, stop=True)
                    S_.ts(sgla[:, hsl], sgla[:, hsl],
                          egl[:, h * NCH + c:h * NCH + c + 1], None, AO.mult)
                    S_.tt(sgla[:, hsl], sgla[:, hsl], bK[0:128, 0:128], AO.add)
                    S_.copy(sglab[:, hsl], sgla[:, hsl])

            for h in range(H):
                S_.act(sq2[:, :], qh2[h][:, :], AF.Square)
                S_.mm(pbC[0:1, 0:512], onecol, qh2[h][:, :], start=True, stop=True)
                S_.mm(pbD[0:1, 0:512], onecol, sq2[:, :], start=True, stop=True)
                mu = rows[0:1, 0:512]
                musq = rows[0:1, 512:1024]
                var = rows[0:1, 1024:1536]
                rstd = rows[0:1, 1536:2048]
                S_.ts(mu, pbC[0:1, 0:512], 1.0 / 128.0, None, AO.mult)
                S_.tt(musq, mu, mu, AO.mult)
                S_.stt(var, pbD[0:1, 0:512], 1.0 / 128.0, musq, AO.mult, AO.subtract)
                S_.act(var, var, AF.Sqrt, bias=ccol[0:1, 2:3])
                S_.recip(rstd, var)
                S_.copy(rowb[0:1, 0:512], rstd)
                S_.stt(musq, mu, -1.0, rstd, AO.mult, AO.mult)
                S_.copy(rowb[0:1, 512:1024], musq)
                S_.mm(pbC[:, 0:512], onerow, rowb[0:1, 0:512], start=True, stop=True)
                S_.mm(pbD[:, 0:512], onerow, rowb[0:1, 512:1024], start=True, stop=True)
                S_.ts(sq2[:, :], pbC[0:128, 0:512], 1.0, None, AO.mult)
                S_.tt(qh2[h][:, :], qh2[h][:, :], sq2[:, :], AO.mult)
                S_.ts(sq2[:, :], pbD[0:128, 0:512], 1.0, None, AO.mult)
                S_.tt(qh2[h][:, :], qh2[h][:, :], sq2[:, :], AO.add)

            def g_consume(oc, tt, bank):
                S_.act(gvt[:, :], bank[:, 0:512], AF.Silu, bias=bgc[:, oc:oc + 1])
                S_.tt(qh2[oc][:, :], qh2[oc][:, :], gvt[:, :], AO.mult)
            project(wgb_in, 16, g_consume, bstr2, tts=(hh,))

            def wob_consume(oc, tt, bank):
                S_.stt(xt[oc][:, hsl_t], bank[:, 0:512], 1.0, xt[oc][:, hsl_t],
                       AO.mult, AO.add)
            project(wob_in, 16, wob_consume, bstr2, rhs=qh2, tts=(0,))

            b2.close()
            S_.free(*[f"p2h{hh}_qh2_{h}" for h in range(H)],
                    *[f"p2h{hh}_kh2_{h}" for h in range(H)],
                    *[f"p2h{hh}_gk2_{h}" for h in range(H)],
                    *[f"p2h{hh}_vt2_{h}" for h in range(H)],
                    f"p2h{hh}_bstr2_0", f"p2h{hh}_bstr2_1", f"p2h{hh}_scr2_0",
                    f"p2h{hh}_scr2_1", f"p2h{hh}_gvt", f"p2h{hh}_sq2")

        b3.close()
        S_.free("sgla", "sglab", "egl",
                *[f"G32_{i}" for i in range(2)], *[f"gch_{i}" for i in range(2)],
                *[f"e32{i}_{j}" for i in range(4) for j in range(2)],
                *[f"mcol_{i}" for i in range(2)], *[f"btm_{i}" for i in range(2)])

    if debug:
        for i in range(NDC):
            S_.dma("sync", dbg["d_xb"][i * 128:(i + 1) * 128, :], xt[i][:, :])

    if upto >= 4:
        ffn(False)
    for i in range(NDC):
        S_.dma("sync", out_d[i * 128:(i + 1) * 128, :], xt[i][:, :])

    final = dict(S_.cnt)
    S_.prog["sync"].append(lambda e, v=final["lds"], h=S_.semh["lds"]: e.wait_ge(h, v))
    S_.prog["scalar"].append(lambda e, v=final["lda"], h=S_.semh["lda"]: e.wait_ge(h, v))
    S_.prog["gpsimd"].append(lambda e, v=final["ldg"], h=S_.semh["ldg"]: e.wait_ge(h, v))

    with nc.Block() as block:
        @block.sync
        def _(eng):
            for fn in S_.prog["sync"]:
                fn(eng)

        @block.scalar
        def _(eng):
            for fn in S_.prog["scalar"]:
                fn(eng)

        @block.tensor
        def _(eng):
            for fn in S_.prog["tensor"]:
                fn(eng)

        @block.vector
        def _(eng):
            for fn in S_.prog["vector"]:
                fn(eng)

        @block.gpsimd
        def _(eng):
            for fn in S_.prog["gpsimd"]:
                fn(eng)

    big.close()
    return nc


def _strips(w):
    O, K = w.shape
    onc = (O + 127) // 128
    if O % 128:
        w = np.pad(w, ((0, onc * 128 - O), (0, 0)))
    a = w.reshape(onc, 128, K // 128, 128)
    return np.ascontiguousarray(a.transpose(0, 3, 2, 1)).reshape(onc, 128, K)


def _pad_heads(w, H=16, dh=64):
    K = w.shape[1]
    out = np.zeros((H * 128, K), w.dtype)
    out.reshape(H, 128, K)[:, :dh] = w.reshape(H, dh, K)
    return out


def make_in_maps(L0_attn_a_x, L0_attn_a_Wq, L0_attn_a_Wk, L0_attn_a_Wv,
                 L0_attn_a_Wo, L0_attn_b_Wq, L0_attn_b_Wk, L0_attn_b_Wkg1,
                 L0_attn_b_Wkg2, L0_attn_b_bkg2, L0_attn_b_Wv, L0_attn_b_Wg,
                 L0_attn_b_bg, L0_attn_b_Wo, L0_ffn_w_gate, L0_ffn_w_up,
                 L0_ffn_w_down):
    f32 = np.float32
    shared = {}
    shared["wqa"] = _strips(np.asarray(L0_attn_a_Wq, f32))
    shared["wka"] = _strips(np.asarray(L0_attn_a_Wk, f32))
    shared["wva"] = _strips(np.asarray(L0_attn_a_Wv, f32))
    shared["woa"] = _strips(np.asarray(L0_attn_a_Wo, f32))
    shared["wqb"] = _strips(_pad_heads(np.asarray(L0_attn_b_Wq, f32)))
    shared["wkb"] = _strips(_pad_heads(np.asarray(L0_attn_b_Wk, f32)))
    shared["wvb"] = _strips(np.asarray(L0_attn_b_Wv, f32))
    shared["wgb"] = _strips(np.asarray(L0_attn_b_Wg, f32))
    shared["wob"] = _strips(np.asarray(L0_attn_b_Wo, f32))
    wkg1 = np.asarray(L0_attn_b_Wkg1, f32)
    shared["wkg1t"] = np.ascontiguousarray(
        wkg1.T.reshape(16, 128, 16).transpose(1, 0, 2).reshape(128, 256))
    wkg2 = np.asarray(L0_attn_b_Wkg2, f32)
    wkg2p = np.zeros((16, 2048), f32)
    wkg2p.reshape(16, 16, 128)[:, :, :64] = wkg2.reshape(16, 64, 16).transpose(2, 0, 1)
    shared["wkg2p"] = wkg2p
    bkg2p = np.zeros((128, 16), f32)
    bkg2p[:64, :] = np.asarray(L0_attn_b_bkg2, f32).reshape(16, 64).T
    shared["bkg2p"] = bkg2p
    shared["bgc"] = np.ascontiguousarray(
        np.asarray(L0_attn_b_bg, f32).reshape(16, 128).T)
    wg = np.asarray(L0_ffn_w_gate, f32)
    wu = np.asarray(L0_ffn_w_up, f32)
    wd = np.asarray(L0_ffn_w_down, f32)
    shared["wgf"] = _strips(wg)
    shared["wuf"] = _strips(wu)
    wdp = np.pad(wd, ((0, 0), (0, I_PAD - I_RAW)))
    aD = wdp.reshape(16, 128, NIC, 128)
    shared["wdf"] = np.ascontiguousarray(aD.transpose(0, 3, 2, 1)).reshape(
        16, 128, NIC * 128)
    shared["ident"] = np.eye(128, dtype=f32).astype(BF)
    shared["masku"] = np.triu(np.ones((128, 128), f32)).astype(BF)
    protm = np.zeros((128, 128), f32)
    for m in range(64):
        protm[m + 64, m] = -1.0
    for m in range(64, 128):
        protm[m - 64, m] = 1.0
    shared["prot"] = protm.astype(BF)
    shared["onecol"] = np.ones((128, 1), f32).astype(BF)
    shared["onerow"] = np.ones((1, 128), f32).astype(BF)

    HD = 128
    inv = 1.0 / 10000.0 ** (np.arange(0, HD, 2, dtype=np.float64) / HD)
    tpos = np.arange(S, dtype=np.float64)
    fr = np.outer(tpos, inv)
    emb = np.concatenate([fr, fr], -1)
    cosf = np.cos(emb).astype(f32)
    sinf = np.sin(emb).astype(f32)

    wsl3 = np.concatenate([wg.reshape(8, -1), wu.reshape(8, -1),
                           wd.reshape(8, -1)], axis=1)

    x = np.asarray(L0_attn_a_x, f32)
    in_maps = []
    for c in range(NCORE):
        b, hf = c // 2, c % 2
        m = dict(shared)
        m["x"] = np.ascontiguousarray(x[b, hf * T:(hf + 1) * T, :].T)
        m["cosr"] = np.ascontiguousarray(cosf[hf * T:(hf + 1) * T, :].T)
        m["sinr"] = np.ascontiguousarray(sinf[hf * T:(hf + 1) * T, :].T)
        sel = np.zeros((128, 8), f32)
        if hf == 1:
            sel[:, c - 1] = 1.0
        m["selw"] = sel
        m["wsl"] = np.ascontiguousarray(wsl3[c].reshape(128, 3 * 12928))
        in_maps.append(m)
    return in_maps


def kernel(**inputs):
    in_maps = make_in_maps(**inputs)
    nc = build_program(False)
    res = run_bass_kernel_spmd(nc, in_maps, core_ids=list(range(NCORE)))
    out = np.empty((B, S, D), np.float32)
    for c in range(NCORE):
        b, hf = c // 2, c % 2
        out[b, hf * T:(hf + 1) * T, :] = np.asarray(res.results[c]["out"]).T
    return out


# version 14
# speedup vs baseline: 1.0186x; 1.0031x over previous
import numpy as np
import ml_dtypes
from contextlib import ExitStack

import concourse.bass as bass
import concourse.mybir as mybir
from concourse.bass_utils import run_bass_kernel_spmd

F32 = mybir.dt.float32
BF16 = mybir.dt.bfloat16
I32 = mybir.dt.int32
AO = mybir.AluOpType
AF = mybir.ActivationFunctionType
AXX = mybir.AxisListType.X

NCORE = 8
B, S, D = 4, 2048, 2048
T = 1024
NDC = D // 128
CH = 128
NCH = T // CH
TW = 512
I_RAW = 6464
I_PAD = 6528
NIC = I_PAD // 128
FFN_CNT = float(I_RAW * D)
BF = ml_dtypes.bfloat16



def _A(x):
    if x is None or isinstance(x, (int, float)):
        return x
    if hasattr(x, "tensor"):
        return x
    try:
        return x[:]
    except Exception:
        return x

class Sched:
    STREAMS = ("sync", "scalar", "tensor", "vector", "gpsimd")

    def __init__(self, nc):
        self.nc = nc
        self.prog = {s: [] for s in self.STREAMS}
        self.semh = {}
        self.cnt = {}
        for name in ("lds", "lda", "ldg", "ve", "se", "te", "gp", "cc"):
            self.semh[name] = nc.alloc_semaphore(name=name)
            self.cnt[name] = 0
        self.wm = {s: {k: 0 for k in self.semh} for s in self.STREAMS}
        self.acc = {}
        self.retired = []
        self.born = {}

    def _names(self, aps):
        out = []
        for a in aps:
            if a is None or isinstance(a, (int, float)):
                continue
            t = getattr(a, "tensor", None)
            if t is None:
                t = a
            out.append(t.name)
        return out

    def _rec(self, name):
        if name not in self.acc:
            self.acc[name] = {"w": None, "r": []}
            self.born[name] = list(self.retired)
        return self.acc[name]

    def _deps(self, reads, writes):
        evs = []
        for n in reads:
            r = self._rec(n)
            if r["w"]:
                evs.append(r["w"])
            if self.born.get(n):
                evs.extend(self.born[n]); self.born[n] = []
        for n in writes:
            r = self._rec(n)
            if r["w"]:
                evs.append(r["w"])
            evs.extend(r["r"])
            if self.born.get(n):
                evs.extend(self.born[n]); self.born[n] = []
        ded = {}
        for s, v in evs:
            ded[s] = max(ded.get(s, 0), v)
        return ded

    def _commit(self, reads, writes, ev):
        for n in reads:
            self._rec(n)["r"].append(ev)
        for n in writes:
            r = self._rec(n)
            r["w"] = ev
            r["r"] = []

    def free(self, *handles):
        for h in handles:
            n = h.name if hasattr(h, "name") else h
            r = self.acc.get(n)
            if r:
                if r["w"]:
                    self.retired.append(r["w"])
                self.retired.extend(r["r"])
        ded = {}
        for s, v in self.retired:
            ded[s] = max(ded.get(s, 0), v)
        self.retired = [(s, v) for s, v in ded.items()]

    def _emit(self, stream, sem, inc, build, reads_aps, writes_aps, extra_ev=None):
        reads = self._names(reads_aps)
        writes = self._names(writes_aps)
        deps = self._deps(reads, writes)
        if extra_ev:
            for s, v in extra_ev:
                deps[s] = max(deps.get(s, 0), v)
        waits = []
        for s, v in deps.items():
            if self.wm[stream][s] < v:
                waits.append((self.semh[s], v))
                self.wm[stream][s] = v
        self.cnt[sem] += inc
        ev = (sem, self.cnt[sem])
        semh = self.semh[sem]

        def fn(eng, waits=waits, build=build, semh=semh, inc=inc):
            for h, v in waits:
                eng.wait_ge(h, v)
            ins = build(eng)
            if ins is not None:
                ins.then_inc(semh, inc)

        self.prog[stream].append(fn)
        self._commit(reads, writes, ev)
        return ev

    def dma(self, stream, out, in_, extra_ev=None, slow_ok=False):
        sem = {"sync": "lds", "scalar": "lda", "gpsimd": "ldg"}[stream]
        if slow_ok:
            return self._emit(stream, sem, 16,
                              lambda e: e.dma_start(out=out, in_=in_,
                                                    allow_slow_non_contiguous=True),
                              [in_], [out], extra_ev=extra_ev)
        return self._emit(stream, sem, 16,
                          lambda e: e.dma_start(out=out, in_=in_), [in_], [out],
                          extra_ev=extra_ev)

    def copy(self, out, in_, eng="vector"):
        out, in_ = _A(out), _A(in_)
        if eng == "scalar":
            return self._emit(eng, "se", 1,
                              lambda e: e.activation(out, in_, AF.Copy), [in_], [out])
        sem = {"vector": "ve", "gpsimd": "gp"}[eng]
        return self._emit(eng, sem, 1, lambda e: e.tensor_copy(out, in_), [in_], [out])

    def tt(self, out, a, b, op=AO.mult, eng="vector"):
        out, a, b = _A(out), _A(a), _A(b)
        sem = {"vector": "ve", "gpsimd": "gp"}[eng]
        return self._emit(eng, sem, 1, lambda e: e.tensor_tensor(out, a, b, op),
                          [a, b], [out])

    def ts(self, out, a, s1, s2=None, op0=AO.mult, op1=AO.bypass, eng="vector"):
        out, a, s1, s2 = _A(out), _A(a), _A(s1), _A(s2)
        sem = {"vector": "ve", "gpsimd": "gp"}[eng]
        rd = [a] + [x for x in (s1, s2) if hasattr(x, "tensor")]
        if s2 is None:
            return self._emit(eng, sem, 1,
                              lambda e: e.tensor_single_scalar(out, a, s1, op0),
                              rd, [out])
        return self._emit(eng, sem, 1,
                          lambda e: e.tensor_scalar(out, a, s1, s2, op0, op1),
                          rd, [out])

    def stt(self, out, in0, scalar, in1, op0, op1, eng="vector"):
        out, in0, scalar, in1 = _A(out), _A(in0), _A(scalar), _A(in1)
        sem = {"vector": "ve", "gpsimd": "gp"}[eng]
        rd = [in0, in1] + ([scalar] if hasattr(scalar, "tensor") else [])
        return self._emit(eng, sem, 1,
                          lambda e: e.scalar_tensor_tensor(out, in0, scalar, in1, op0, op1),
                          rd, [out])

    def scan(self, out, d0, d1, initial, op0, op1):
        out, d0, d1 = _A(out), _A(d0), _A(d1)
        rd = [d0, d1] + ([initial] if hasattr(initial, "tensor") else [])
        return self._emit("vector", "ve", 1,
                          lambda e: e.tensor_tensor_scan(out, d0, d1, initial, op0, op1),
                          rd, [out])

    def red(self, out, in_, op=AO.add, absval=False, negate=False):
        out, in_ = _A(out), _A(in_)
        def b(e):
            return e.tensor_reduce(out, in_, axis=AXX, op=op,
                                   apply_absolute_value=absval, negate=negate)
        return self._emit("vector", "ve", 1, b, [in_], [out])

    def recip(self, out, in_):
        out, in_ = _A(out), _A(in_)
        return self._emit("vector", "ve", 1, lambda e: e.reciprocal(out, in_),
                          [in_], [out])

    def act(self, out, in_, func, bias=0.0, scale=1.0):
        out, in_, bias, scale = _A(out), _A(in_), _A(bias), _A(scale)
        rd = [in_] + [x for x in (bias, scale) if hasattr(x, "tensor")]
        return self._emit("scalar", "se", 1,
                          lambda e: e.activation(out, in_, func, bias=bias, scale=scale),
                          rd, [out])

    def memset(self, ap, val, eng="vector"):
        ap = _A(ap)
        sem = {"vector": "ve", "gpsimd": "gp"}[eng]
        return self._emit(eng, sem, 1, lambda e: e.memset(ap, val), [], [ap])

    def mm(self, out, lhsT, rhs, start, stop, inc=True):
        out, lhsT, rhs = _A(out), _A(lhsT), _A(rhs)
        if not hasattr(self, "_g"):
            self._g = None
        def b(e, out=out, lhsT=lhsT, rhs=rhs, start=start, stop=stop):
            return e.matmul(out, lhsT, rhs, start=start, stop=stop)
        if self._g is None and start and stop:
            return self._emit("tensor", "te", 1, b, [lhsT, rhs], [out])
        if self._g is None:
            self._g = []
        self._g.append((b, [lhsT, rhs], [out]))
        if not stop:
            return None
        group = self._g
        self._g = None
        deps = {}
        for _, rd, wr in group:
            d = self._deps(self._names(rd), self._names(wr))
            for s, v in d.items():
                deps[s] = max(deps.get(s, 0), v)
        waits = []
        for s, v in deps.items():
            if self.wm["tensor"][s] < v:
                waits.append((self.semh[s], v))
                self.wm["tensor"][s] = v
        self.cnt["te"] += 1
        ev = ("te", self.cnt["te"])
        semh = self.semh["te"]

        def fn(eng, waits=waits, group=group, semh=semh):
            for h, v in waits:
                eng.wait_ge(h, v)
            ins = None
            for gb, _, _ in group:
                ins = gb(eng)
            ins.then_inc(semh, 1)

        self.prog["tensor"].append(fn)
        for _, rd, wr in group:
            self._commit(self._names(rd), self._names(wr), ev)
        return ev

    def transpose(self, out_psum, in_sb, ident):
        out_psum, in_sb, ident = _A(out_psum), _A(in_sb), _A(ident)
        kp = in_sb.shape[0]
        def b(e):
            return e.matmul(out_psum, in_sb, ident[0:kp, 0:kp], start=True, stop=True,
                            is_transpose=True)
        return self._emit("tensor", "te", 1, b, [in_sb, ident], [out_psum])

    def collective(self, kind, op, rg, ins, outs):
        return self._emit("gpsimd", "cc", 1,
                          lambda e: e.collective_compute(kind, op, rg, ins, outs),
                          ins, outs)


def build_program(debug=False, upto=99, asub=9):
    nc = bass.Bass("TRN2", target_bir_lowering=False, debug=False)
    S_ = Sched(nc)
    rg = [list(range(NCORE))]

    din = {}
    def IN(name, shape, dt=F32):
        din[name] = nc.dram_tensor(name, shape, dt, kind="ExternalInput").ap()
        return din[name]

    x_in = IN("x", [D, T])
    wqa_in = IN("wqa", [16, 128, 2048])
    wka_in = IN("wka", [4, 128, 2048])
    wva_in = IN("wva", [4, 128, 2048])
    woa_in = IN("woa", [16, 128, 2048])
    wqb_in = IN("wqb", [16, 128, 2048])
    wkb_in = IN("wkb", [16, 128, 2048])
    wvb_in = IN("wvb", [16, 128, 2048])
    wgb_in = IN("wgb", [16, 128, 2048])
    wob_in = IN("wob", [16, 128, 2048])
    wkg1_in = IN("wkg1t", [128, 256])
    wkg2_in = IN("wkg2p", [16, 2048])
    bkg2_in = IN("bkg2p", [128, 16])
    bg_in = IN("bgc", [128, 16])
    wg_in = IN("wgf", [NIC, 128, 2048])
    wu_in = IN("wuf", [NIC, 128, 2048])
    wd_in = IN("wdf", [16, 128, NIC * 128])
    wsl_in = IN("wsl", [128, 3 * 12928])
    ident_in = IN("ident", [128, 128], BF16)
    masku_in = IN("masku", [128, 128], BF16)
    prot_in = IN("prot", [128, 128], BF16)
    onecol_in = IN("onecol", [128, 1], BF16)
    onerow_in = IN("onerow", [1, 128], BF16)
    cos_in = IN("cosr", [128, T])
    sin_in = IN("sinr", [128, T])
    selw_in = IN("selw", [128, 8])
    out_d = nc.dram_tensor("out", [D, T], F32, kind="ExternalOutput").ap()
    dbg = {}
    if debug:
        for nm in ("d_ln1", "d_xa", "d_xf1", "d_xb"):
            dbg[nm] = nc.dram_tensor(nm, [D, T], F32, kind="ExternalOutput").ap()

    ar_in = nc.dram_tensor("ar_in", [1, 3], F32, kind="Internal").ap()
    ar_out = nc.dram_tensor("ar_out", [1, 3], F32, kind="Internal",
                            addr_space="Shared").ap()
    acc_d = nc.dram_tensor("acc_d", [128, 3], F32, kind="Internal").ap()
    scales_d = nc.dram_tensor("scales_d", [1, 9], F32, kind="Internal").ap()
    qdt = [nc.dram_tensor(f"qdt{i}", [1, 4, 128], F32, kind="Internal").ap()
           for i in range(3)]
    aga_in = nc.dram_tensor("aga_in", [128, 528], BF16, kind="Internal").ap()
    aga_out = nc.dram_tensor("aga_out", [NCORE, 128, 528], BF16, kind="Internal",
                             addr_space="Shared").ap()
    agb_in = nc.dram_tensor("agb_in", [64, 2048], BF16, kind="Internal").ap()
    agb_out = nc.dram_tensor("agb_out", [NCORE, 64, 2048], BF16, kind="Internal",
                             addr_space="Shared").ap()
    terng_d = nc.dram_tensor("terng_d", [NIC, 128, 2048], BF16, kind="Internal").ap()
    ternu_d = nc.dram_tensor("ternu_d", [NIC, 128, 2048], BF16, kind="Internal").ap()
    ternd_d = nc.dram_tensor("ternd_d", [16, 128, NIC * 128], BF16, kind="Internal").ap()
    kh_d = nc.dram_tensor("kh_d", [16, 128, T], BF16, kind="Internal").ap()
    gk_d = nc.dram_tensor("gk_d", [16, 128, T], BF16, kind="Internal").ap()
    vt_d = nc.dram_tensor("vt_d", [16, 128, T], BF16, kind="Internal").ap()

    big = ExitStack()
    def SB(name, shape, dt):
        return big.enter_context(nc.sbuf_tensor(name, shape, dt))

    xt = [SB(f"xt{i}", [128, T], F32) for i in range(NDC)]
    ln = [SB(f"ln{i}", [128, T], BF16) for i in range(NDC)]
    ident = SB("identS", [128, 128], BF16)
    masku = SB("maskuS", [128, 128], BF16)
    prot = SB("protS", [128, 128], BF16)
    onecol = SB("onecolS", [128, 1], BF16)
    onerow = SB("onerowS", [1, 128], BF16)
    cosb = SB("cosb", [128, T], BF16)
    sinb = SB("sinb", [128, T], BF16)
    selw = SB("selwS", [128, 8], F32)
    ccol = SB("ccol", [128, 8], F32)
    scol = SB("scol", [128, 12], F32)
    bkg2c = SB("bkg2cS", [128, 16], F32)
    bgc = SB("bgcS", [128, 16], F32)
    rows = SB("rows", [1, 2048], F32)
    rowb = SB("rowb", [1, 1024], BF16)
    ones32 = SB("ones32", [128, 128], F32)

    NHALF, PHALF, EPS5, EPS6, NEG1 = (ccol[:, i:i + 1] for i in range(5))

    pb = [big.enter_context(nc.psum_tensor(f"pb{i}", [128, 512], F32))
          for i in range(6)]
    pbA, pbB, pbC, pbD, pbE, pbF = pb
    ptA = big.enter_context(nc.psum_tensor("ptA", [128, 256], BF16))
    ptB = big.enter_context(nc.psum_tensor("ptB", [128, 256], BF16))

    S_.memset(ccol[:, 0:1], -0.5)
    S_.memset(ccol[:, 1:2], 0.5)
    S_.memset(ccol[:, 2:3], 1e-5)
    S_.memset(ccol[:, 3:4], 1e-6)
    S_.memset(ccol[:, 4:5], -1.0)
    S_.memset(ones32[:, :], 1.0)
    for i in range(NDC):
        S_.dma("sync", xt[i][:, :], x_in[i * 128:(i + 1) * 128, :])
    S_.dma("sync", ident[:, :], ident_in)
    S_.dma("sync", masku[:, :], masku_in)
    S_.dma("sync", prot[:, :], prot_in)
    S_.dma("sync", onecol[:, :], onecol_in)
    S_.dma("sync", onerow[:, :], onerow_in)
    S_.dma("sync", selw[:, :], selw_in)
    S_.dma("sync", bkg2c[:, :], bkg2_in)
    S_.dma("sync", bgc[:, :], bg_in)

    p0 = ExitStack()
    def P0(name, shape, dt):
        return p0.enter_context(nc.sbuf_tensor(name, shape, dt))
    cs32 = P0("cs32", [128, T], F32)
    S_.dma("sync", cs32[:, :], cos_in)
    S_.copy(cosb[:, :], cs32[:, :])
    S_.dma("sync", cs32[:, :], sin_in)
    S_.copy(sinb[:, :], cs32[:, :])

    wslb = [P0(f"wslb{i}", [128, 6464], F32) for i in range(2)]
    acc3 = P0("acc3", [128, 4], F32)
    rc1 = P0("rc1", [128, 1], F32)
    for mi in range(3):
        for hf in range(2):
            buf = wslb[hf]
            sl = slice(mi * 12928 + hf * 6464, mi * 12928 + (hf + 1) * 6464)
            S_.dma("scalar", buf[:, :], wsl_in[:, sl])
            if hf == 0:
                S_.red(acc3[:, mi:mi + 1], buf[:, :], AO.add, absval=True)
            else:
                S_.red(rc1[:, :], buf[:, :], AO.add, absval=True)
                S_.tt(acc3[:, mi:mi + 1], acc3[:, mi:mi + 1], rc1[:, :], AO.add)
    accrow = P0("accrow", [1, 384], F32)
    S_.dma("sync", acc_d, acc3[:, 0:3])
    S_.dma("sync", accrow[0:1, :], acc_d)
    for mi in range(3):
        S_.red(rows[0:1, 16 + mi:17 + mi], accrow[0:1, mi::3], AO.add)
    S_.dma("sync", ar_in, rows[0:1, 16:19])
    S_.collective("AllReduce", AO.add, rg, [ar_in], [ar_out])
    S_.dma("sync", rows[0:1, 0:3], ar_out)
    S_.ts(rows[0:1, 8:11], rows[0:1, 0:3], 1.0 / FFN_CNT, 1e-5, AO.mult, AO.max)
    S_.recip(rows[0:1, 11:14], rows[0:1, 8:11])
    S_.ts(rows[0:1, 14:17], rows[0:1, 8:11], 0.5, None, AO.mult)
    S_.dma("sync", scales_d, rows[0:1, 8:17])
    S_.dma("sync", scol[:, 0:9], scales_d.broadcast_to([128, 9]))
    p0.close()
    S_.free("cs32", "wslb0", "wslb1", "acc3", "rc1", "accrow")

    lnscr = [SB(f"lnscr{i}", [128, 512], BF16) for i in range(2)]
    ln_n = [0]

    def layernorm():
        ln_n[0] += 1
        les = ExitStack()
        sqt = [les.enter_context(nc.sbuf_tensor(f"lnsq{ln_n[0]}_{i}", [128, 512], BF16))
               for i in range(NDC)]
        for tt in range(2):
            tsl = slice(tt * TW, (tt + 1) * TW)
            for i in range(NDC):
                S_.copy(ln[i][:, tsl], xt[i][:, tsl],
                        eng=("vector" if i % 2 == 0 else "vector"))
            for i in range(NDC):
                S_.mm(pbC[0:1, 0:512], onecol, ln[i][:, tsl],
                      start=(i == 0), stop=(i == NDC - 1), inc=(i == NDC - 1))
            for i in range(NDC):
                S_.act(sqt[i][:, :], ln[i][:, tsl], AF.Square)
            for i in range(NDC):
                S_.mm(pbD[0:1, 0:512], onecol, sqt[i][:, :],
                      start=(i == 0), stop=(i == NDC - 1), inc=(i == NDC - 1))
            mu = rows[0:1, 0:512]
            musq = rows[0:1, 512:1024]
            var = rows[0:1, 1024:1536]
            rstd = rows[0:1, 1536:2048]
            S_.ts(mu, pbC[0:1, 0:512], 1.0 / D, None, AO.mult)
            S_.tt(musq, mu, mu, AO.mult)
            S_.stt(var, pbD[0:1, 0:512], 1.0 / D, musq, AO.mult, AO.subtract)
            S_.act(var, var, AF.Sqrt, bias=ccol[0:1, 2:3])
            S_.recip(rstd, var)
            S_.copy(rowb[0:1, 0:512], rstd)
            S_.stt(musq, mu, -1.0, rstd, AO.mult, AO.mult)
            S_.copy(rowb[0:1, 512:1024], musq)
            S_.mm(pbC[:, 0:512], onerow, rowb[0:1, 0:512], start=True, stop=True)
            S_.mm(pbD[:, 0:512], onerow, rowb[0:1, 512:1024], start=True, stop=True)
            S_.ts(lnscr[0][:, :], pbC[0:128, 0:512], 1.0, None, AO.mult)
            S_.ts(lnscr[1][:, :], pbD[0:128, 0:512], 1.0, None, AO.mult)
            for i in range(NDC):
                S_.tt(ln[i][:, tsl], ln[i][:, tsl], lnscr[0][:, :], AO.mult)
                S_.tt(ln[i][:, tsl], ln[i][:, tsl], lnscr[1][:, :], AO.add)
        les.close()
        S_.free(*[f"lnsq{ln_n[0]}_{i}" for i in range(NDC)])

    def project(strips_ap, n_oc, consumer, bufs, bank_of=None, rhs=None, tts=(0, 1)):
        rhs = rhs or ln
        for oc in range(n_oc):
            buf = bufs[oc % len(bufs)]
            S_.dma("gpsimd", buf[:, :], strips_ap[oc])
            for tt in tts:
                tsl = slice(tt * TW, (tt + 1) * TW)
                bank = (bank_of((oc * len(tts) + (tt - tts[0])) % 2) if bank_of
                        else pb[(oc * len(tts) + tt) % 2])
                for dc in range(NDC):
                    S_.mm(bank[:, 0:512], buf[:, dc * 128:(dc + 1) * 128],
                          rhs[dc][:, tsl], start=(dc == 0), stop=(dc == NDC - 1),
                          inc=(dc == NDC - 1))
                consumer(oc, tt, bank)

    if upto >= 1:
        layernorm()

    if upto >= 1:
        aa = ExitStack()
        def AA(name, shape, dt):
            return aa.enter_context(nc.sbuf_tensor(name, shape, dt))

        qf = [AA(f"qf{h}", [128, T], BF16) for h in range(16)]
        kf = [AA(f"kf{g}", [128, T], BF16) for g in range(4)]
        vtm = [AA(f"vtm{g}", [128, NCH * 132], BF16) for g in range(4)]
        srun = AA("srun", [128, 4 * 132], F32)
        sbfa = AA("sbfa", [128, 4 * 132], BF16)
        rsc = [AA(f"rsc{i}", [128, 512], BF16) for i in range(6)]
        ktm = AA("ktm", [128, 128], BF16)
        ktm2 = AA("ktm2", [128, 128], BF16)
        rowsB = AA("rowsB", [1, 1024], F32)
        rowbB = AA("rowbB", [1, 512], BF16)
        strips = [AA(f"strip{i}", [128, 2048], BF16) for i in range(3)]

        def rope_elu(dst, oc, tt, bank):
            tsl = slice(tt * TW, (tt + 1) * TW)
            S_.act(rsc[0][:, :], bank[:, 0:512], AF.Copy)
            S_.mm(pbC[:, 0:512], prot, rsc[0][:, :], start=True, stop=True)
            S_.tt(rsc[1][:, :], rsc[0][:, :], cosb[:, tsl], AO.mult)
            S_.stt(rsc[2][:, :], pbC[0:128, 0:512], 1.0, sinb[:, tsl], AO.mult, AO.mult)
            S_.tt(rsc[2][:, :], rsc[2][:, :], rsc[1][:, :], AO.add)
            S_.ts(rsc[3][:, :], rsc[2][:, :], 0.0, None, AO.min)
            S_.act(rsc[4][:, :], rsc[3][:, :], AF.Exp)
            S_.stt(dst[oc][:, tsl], rsc[2][:, :], 0.0, rsc[4][:, :], AO.max, AO.add)

        if asub >= 1:
            project(wka_in, 4, lambda oc, tt, bank: rope_elu(kf, oc, tt, bank), strips)

        for g in range(4):
            S_.memset(vtm[g][:, :], 1.0)

        def v_consume(oc, tt, bank):
            S_.act(rsc[5][:, :], bank[:, 0:512], AF.Copy)
            for j in range(4):
                c = tt * 4 + j
                ptj = ptA if j % 2 == 0 else ptB
                S_.transpose(ptj[:, 0:128], rsc[5][:, j * 128:(j + 1) * 128], ident)
                S_.copy(vtm[oc][:, c * 132:c * 132 + 128], ptj[0:128, 0:128], eng="scalar")
        if asub >= 1:
            project(wva_in, 4, v_consume, strips)
            project(wqa_in, 16, lambda oc, tt, bank: rope_elu(qf, oc, tt, bank), strips)

        def a_state_mm(g, c):
            csl = slice(c * CH, (c + 1) * CH)
            ptj = ptA if g % 2 == 0 else ptB
            ktmj = ktm if g % 2 == 0 else ktm2
            S_.transpose(ptj[:, 128:256], kf[g][:, csl], ident)
            S_.copy(ktmj[:, :], ptj[0:128, 128:256], eng="scalar")
            S_.mm(pbF[:, 0:129], ktmj, vtm[g][:, c * 132:c * 132 + 129],
                  start=True, stop=True)

        if asub >= 2:
            for c in range(NCH):
                for g in range(4):
                    a_state_mm(g, c)
                    ssl = slice(g * 132, g * 132 + 129)
                    if c == 0:
                        S_.ts(srun[:, ssl], pbF[0:128, 0:129], 1.0, None, AO.mult)
                    else:
                        S_.tt(srun[:, ssl], srun[:, ssl], pbF[0:128, 0:129], AO.add)
            S_.copy(sbfa[:, :], srun[:, :])
        else:
            S_.memset(srun[:, :], 0.0)
            S_.copy(sbfa[:, :], srun[:, :])
        if asub >= 3:
            S_.dma("sync", aga_in, sbfa[:, 0:528])
            S_.collective("AllGather", AO.bypass, rg, [aga_in], [aga_out])

        agr = [AA(f"agr{i}", [128, 528], BF16) for i in range(2)]
        for r in (range(NCORE) if asub >= 3 else []):
            buf = agr[r % 2]
            S_.dma("sync", buf[:, :], aga_out[r])
            if r == 0:
                S_.ts(srun[:, :], buf[:, :], selw[:, 0:1], None, AO.mult)
            else:
                S_.stt(srun[:, :], buf[:, :], selw[:, r:r + 1], srun[:, :],
                       AO.mult, AO.add)
        S_.copy(sbfa[:, :], srun[:, :])

        for c in range(NCH):
            csl = slice(c * CH, (c + 1) * CH)
            for g in range(4):
                hs = [4 * g + i for i in range(4)]
                bA_ = pbD if g % 2 == 0 else pbC
                bN_ = pbE if g % 2 == 0 else pbF
                bDen = pbA if g % 2 == 0 else pbB
                rws = rows if g % 2 == 0 else rowsB
                rwb = rowb if g % 2 == 0 else rowbB
                rbc = rsc[4] if g % 2 == 0 else rsc[5]
                for i, h in enumerate(hs):
                    S_.mm(bA_[:, i * 128:(i + 1) * 128], kf[g][:, csl], qf[h][:, csl],
                          start=(i == 0), stop=(i == 3), inc=(i == 3))
                for i, h in enumerate(hs):
                    S_.tt(rsc[i][:, 0:128], bA_[0:128, i * 128:(i + 1) * 128],
                          masku, AO.mult)
                for i, h in enumerate(hs):
                    S_.mm(bN_[:, i * 128:(i + 1) * 128], vtm[g][:, c * 132:c * 132 + 128],
                          rsc[i][:, 0:128], start=(i == 0), stop=False, inc=False)
                    S_.mm(bN_[:, i * 128:(i + 1) * 128], sbfa[:, g * 132:g * 132 + 128],
                          qf[h][:, csl], start=False, stop=(i == 3), inc=(i == 3))
                for i, h in enumerate(hs):
                    S_.mm(bDen[0:1, i * 128:(i + 1) * 128], onecol, rsc[i][:, 0:128],
                          start=(i == 0), stop=False, inc=False)
                    S_.mm(bDen[0:1, i * 128:(i + 1) * 128],
                          sbfa[:, g * 132 + 128:g * 132 + 129], qf[h][:, csl],
                          start=False, stop=(i == 3), inc=(i == 3))
                S_.ts(rws[0:1, 0:512], bDen[0:1, 0:512], 1e-6, None, AO.add)
                S_.recip(rws[0:1, 512:1024], rws[0:1, 0:512])
                S_.copy(rwb[0:1, 0:512], rws[0:1, 512:1024])
                S_.mm(bA_[:, 0:512], onerow, rwb[0:1, 0:512], start=True, stop=True)
                S_.ts(rbc[:, :], bA_[0:128, 0:512], 1.0, None, AO.mult)
                for i, h in enumerate(hs):
                    S_.tt(qf[h][:, csl], bN_[0:128, i * 128:(i + 1) * 128],
                          rbc[:, i * 128:(i + 1) * 128], AO.mult)
            if c < NCH - 1:
                for g in range(4):
                    a_state_mm(g, c)
                    ssl = slice(g * 132, g * 132 + 129)
                    S_.tt(srun[:, ssl], srun[:, ssl], pbF[0:128, 0:129], AO.add)
                    S_.copy(sbfa[:, ssl], srun[:, ssl])

        def woa_consume(oc, tt, bank):
            tsl = slice(tt * TW, (tt + 1) * TW)
            S_.stt(xt[oc][:, tsl], bank[:, 0:512], 1.0, xt[oc][:, tsl], AO.mult, AO.add)
        if asub >= 5:
            project(woa_in, 16, woa_consume, strips, rhs=qf)

        aa.close()
        S_.free(*[f"qf{h}" for h in range(16)], *[f"kf{g}" for g in range(4)],
                *[f"vtm{g}" for g in range(4)], "srun", "sbfa",
                *[f"rsc{i}" for i in range(6)], "ktm", "strip0", "strip1",
                "agr0", "agr1")
        if debug:
            for i in range(NDC):
                S_.dma("sync", dbg["d_xa"][i * 128:(i + 1) * 128, :], xt[i][:, :])

    def ffn(first):
        layernorm()
        fs = ExitStack()
        pfx = "fA_" if first else "fB_"
        def FF(name, shape, dt):
            return fs.enter_context(nc.sbuf_tensor(pfx + name, shape, dt))
        hbuf = [FF(f"h{ic}", [128, 512], BF16) for ic in range(NIC)]
        fsc = [FF(f"fsc{i}", [128, 512], BF16) for i in range(2)]
        hgt = FF("hgt", [128, 512], BF16)
        hut = FF("hut", [128, 512], BF16)
        sxb = FF("sxb", [128, 512], F32)
        fb = [FF(f"fb{i}", [128, 512], F32) for i in range(2)]
        q32 = FF("q32", [128, 512], F32)
        qi = FF("qi", [128, 512], I32)
        nbg = 2 if first else 4
        nbu = 2 if first else 3
        if first:
            wf32 = [FF(f"wf32{i}", [128, 1024], F32) for i in range(2)]
            sg2 = [FF(f"sg2{i}", [128, 1024], BF16) for i in range(2)]
        terng = [FF(f"terng{i}", [128, 2048], BF16) for i in range(nbg)]
        ternu = [FF(f"ternu{i}", [128, 2048], BF16) for i in range(nbu)]

        def quant_rows(tiles, n, scis, outs):
            S_.tt(fsc[0][:, :], tiles[0], tiles[1], AO.max)
            S_.tt(fsc[1][:, :], tiles[0], tiles[1], AO.min)
            for i in range(2, n):
                S_.tt(fsc[0][:, :], fsc[0][:, :], tiles[i], AO.max)
                S_.tt(fsc[1][:, :], fsc[1][:, :], tiles[i], AO.min)
            S_.ts(fsc[1][:, :], fsc[1][:, :], -1.0, None, AO.mult)
            S_.tt(fsc[0][:, :], fsc[0][:, :], fsc[1][:, :], AO.max)
            for j in range(4):
                ptj = ptA if j % 2 == 0 else ptB
                S_.transpose(ptj[:, (j // 2) * 128:(j // 2) * 128 + 128],
                             fsc[0][:, j * 128:(j + 1) * 128], ident)
            for j in range(4):
                ptj = ptA if j % 2 == 0 else ptB
                S_.red(q32[:, j:j + 1], ptj[0:128, (j // 2) * 128:(j // 2) * 128 + 128],
                       AO.max)
            S_.ts(q32[:, 4:8], q32[:, 0:4], 1e-5, 1.0 / 127.0, AO.max, AO.mult)
            S_.recip(q32[:, 8:12], q32[:, 4:8])
            S_.dma("sync", qdt[0].rearrange("a j p -> p (a j)"), q32[:, 8:12], slow_ok=True)
            for i, sci in enumerate(scis):
                S_.ts(q32[:, 12 + 4 * i:16 + 4 * i], q32[:, 4:8],
                      scol[:, 6 + sci:7 + sci], None, AO.mult)
                S_.dma("sync", qdt[1 + i].rearrange("a j p -> p (a j)"),
                       q32[:, 12 + 4 * i:16 + 4 * i], slow_ok=True)
            for i, o in enumerate(outs):
                S_.dma("sync", o[:, :],
                       qdt[i].rearrange("a j p -> a (j p)").broadcast_to([128, 512]))

        def quantize(tile_ap):
            S_.tt(q32[:, :], tile_ap, sxb[:, :], AO.mult)
            S_.act(qi[:, :], q32[:, :], AF.Copy)
            S_.copy(tile_ap, qi[:, :])

        def tern_load(dst, src_ap, width, sci):
            nh = (width + 1023) // 1024
            for hf in range(nh):
                w = min(1024, width - hf * 1024)
                buf = wf32[hf % 2]
                S_.dma("scalar", buf[:, 0:w], src_ap[:, hf * 1024:hf * 1024 + w])
                S_.act(sg2[0][:, 0:w], buf[:, 0:w], AF.Sign, bias=NHALF,
                       scale=scol[:, 3 + sci:4 + sci])
                S_.act(sg2[1][:, 0:w], buf[:, 0:w], AF.Sign, bias=PHALF,
                       scale=scol[:, 3 + sci:4 + sci])
                S_.tt(dst[:, hf * 1024:hf * 1024 + w], sg2[0][:, 0:w],
                      sg2[1][:, 0:w], AO.add)

        for tt in range(2):
            tsl = slice(tt * TW, (tt + 1) * TW)
            lslices = [ln[i][:, tsl] for i in range(NDC)]
            quant_rows(lslices, NDC, (0, 1), (sxb, fb[0], fb[1]))
            for i in range(NDC):
                quantize(ln[i][:, tsl])
            for oc in range(NIC):
                tg, tu = terng[oc % nbg], ternu[oc % nbu]
                if first and tt == 0:
                    tern_load(tg, wg_in[oc], 2048, 0)
                    S_.dma("sync", terng_d[oc], tg[:, :])
                    tern_load(tu, wu_in[oc], 2048, 1)
                    S_.dma("sync", ternu_d[oc], tu[:, :])
                else:
                    S_.dma("scalar" if oc % 2 == 0 else "sync", tg[:, :], terng_d[oc])
                    S_.dma("scalar" if oc % 2 == 1 else "sync", tu[:, :], ternu_d[oc])
                bg_, bu_ = [(pbA, pbB), (pbC, pbD), (pbE, pbF)][oc % 3]
                for dc in range(NDC):
                    S_.mm(bg_[:, 0:512], tg[:, dc * 128:(dc + 1) * 128],
                          ln[dc][:, tsl], start=(dc == 0), stop=(dc == NDC - 1),
                          inc=(dc == NDC - 1))
                for dc in range(NDC):
                    S_.mm(bu_[:, 0:512], tu[:, dc * 128:(dc + 1) * 128],
                          ln[dc][:, tsl], start=(dc == 0), stop=(dc == NDC - 1),
                          inc=(dc == NDC - 1))
                S_.tt(q32[:, :], bg_[0:128, 0:512], fb[0][:, :], AO.mult)
                S_.act(hgt[:, :], q32[:, :], AF.Silu)
                S_.tt(hut[:, :], bu_[0:128, 0:512], fb[1][:, :], AO.mult)
                S_.tt(hbuf[oc][:, :], hgt[:, :], hut[:, :], AO.mult)
            hs = [hbuf[i][:, :] for i in range(NIC)]
            quant_rows(hs, NIC, (2,), (sxb, fb[0]))
            for ic in range(NIC):
                quantize(hbuf[ic][:, :])
            for oc in range(NDC):
                bd_ = pbE if oc % 2 == 0 else pbF
                for qtr in range(4):
                    k0 = qtr * 13
                    nk = 13 if qtr < 3 else NIC - 39
                    buf = [terng[0], ternu[0], terng[1], ternu[1]][qtr]
                    if first and tt == 0:
                        tern_load(buf[:, 0:nk * 128],
                                  wd_in[oc][:, k0 * 128:(k0 + nk) * 128],
                                  nk * 128, 2)
                        S_.dma("sync", ternd_d[oc][:, k0 * 128:(k0 + nk) * 128],
                               buf[:, 0:nk * 128])
                    else:
                        S_.dma("scalar", buf[:, 0:nk * 128],
                               ternd_d[oc][:, k0 * 128:(k0 + nk) * 128])
                    for k in range(nk):
                        kc = k0 + k
                        S_.mm(bd_[:, 0:512], buf[:, k * 128:(k + 1) * 128],
                              hbuf[kc][:, :], start=(kc == 0), stop=(kc == NIC - 1),
                              inc=(kc == NIC - 1))
                S_.tt(q32[:, :], bd_[0:128, 0:512], fb[0][:, :], AO.mult)
                S_.tt(xt[oc][:, tsl], xt[oc][:, tsl], q32[:, :], AO.add)
        fs.close()
        S_.free(*[pfx + f"h{ic}" for ic in range(NIC)],
                *[pfx + f"fsc{i}" for i in range(2)],
                pfx + "hgt", pfx + "hut", pfx + "sxb", pfx + "fb0", pfx + "fb1",
                pfx + "q32", pfx + "qi",
                *[pfx + f"terng{i}" for i in range(nbg)],
                *[pfx + f"ternu{i}" for i in range(nbu)],
                *([pfx + "sg20", pfx + "sg21", pfx + "wf320", pfx + "wf321"]
                  if first else []))

    if upto >= 2:
        ffn(True)
    if debug:
        for i in range(NDC):
            S_.dma("sync", dbg["d_xf1"][i * 128:(i + 1) * 128, :], xt[i][:, :])

    if upto >= 3:
        layernorm()
        H = 16

        b3 = ExitStack()
        def B3(name, shape, dt):
            return b3.enter_context(nc.sbuf_tensor(name, shape, dt))
        sgla = B3("sgla", [128, H * 128], F32)
        sglab = B3("sglab", [128, H * 128], BF16)
        egl = B3("egl", [128, H * NCH], F32)
        G32p = [B3(f"G32_{i}", [128, 128], F32) for i in range(2)]
        gchp = [B3(f"gch_{i}", [128, 128], F32) for i in range(2)]
        e32p = [[B3(f"e32{i}_{j}", [128, 128], BF16) for i in range(4)]
                for j in range(2)]
        mcolp = [B3(f"mcol_{i}", [128, 4], F32) for i in range(2)]
        btmp = [B3(f"btm_{i}", [128, 128], BF16) for i in range(2)]

        S_.memset(sgla[:, :], 0.0)

        for hh in range(2):
            hsl_t = slice(hh * 512, (hh + 1) * 512)
            b1 = ExitStack()
            def B1(name, shape, dt):
                return b1.enter_context(nc.sbuf_tensor(f"p1h{hh}_" + name, shape, dt))
            khb = [B1(f"khb{h}", [128, 512], BF16) for h in range(H)]
            gkb = [B1(f"gkb{h}", [128, 512], BF16) for h in range(H)]
            vtb = [B1(f"vtb{h}", [128, 512], BF16) for h in range(H)]
            bstr = [B1(f"bstr{i}", [128, 2048], BF16) for i in range(2)]
            kg1 = B1("kg1", [16, 512], BF16)
            wkg1s = B1("wkg1s", [128, 256], BF16)
            wkg2s = B1("wkg2s", [16, 2048], BF16)
            sg2v = B1("sg2v", [128, 512], BF16)

            def k_consume(oc, tt, bank):
                S_.ts(khb[oc][:, :], bank[:, 0:512], float(128.0 ** -0.5), None, AO.mult)
            project(wkb_in, 16, k_consume, bstr, tts=(hh,))

            def vb_consume(oc, tt, bank):
                S_.act(sg2v[:, :], bank[:, 0:512], AF.Copy)
                for j in range(4):
                    ptj = ptA if j % 2 == 0 else ptB
                    S_.transpose(ptj[:, 0:128], sg2v[:, j * 128:(j + 1) * 128], ident)
                    S_.copy(vtb[oc][:, j * 128:(j + 1) * 128], ptj[0:128, 0:128],
                            eng="scalar")
            project(wvb_in, 16, vb_consume, bstr, tts=(hh,))

            S_.dma("gpsimd", wkg1s[:, :], wkg1_in)
            S_.dma("gpsimd", wkg2s[:, :], wkg2_in)
            for dc in range(NDC):
                S_.mm(pbC[0:16, 0:512], wkg1s[:, dc * 16:(dc + 1) * 16],
                      ln[dc][:, hsl_t], start=(dc == 0), stop=(dc == NDC - 1),
                      inc=(dc == NDC - 1))
            S_.copy(kg1[:, :], pbC[0:16, 0:512])
            for h in range(H):
                S_.mm(pbD[:, 0:512], wkg2s[:, h * 128:(h + 1) * 128], kg1[:, :],
                      start=True, stop=True)
                S_.act(sg2v[:, :], pbD[0:128, 0:512], AF.Sigmoid, bias=bkg2c[:, h:h + 1])
                S_.act(gkb[h][:, :], sg2v[:, :], AF.Ln, scale=1.0)

            for cc in range(4):
                c = hh * 4 + cc
                csl = slice(cc * CH, (cc + 1) * CH)
                for h in range(H):
                    hsl = slice(h * 128, (h + 1) * 128)
                    G32, gch, e32 = G32p[h % 2], gchp[h % 2], e32p[h % 2]
                    mcol, btm = mcolp[h % 2], btmp[h % 2]
                    S_.ts(gch[:, :], gkb[h][:, csl], 1.0 / 16.0, None, AO.mult)
                    S_.scan(G32[:, :], ones32, gch, 0.0, AO.mult, AO.add)
                    S_.ts(mcol[:, 0:1], G32[:, 127:128], 1.0, None, AO.mult)
                    S_.act(egl[:, h * NCH + c:h * NCH + c + 1], mcol[:, 0:1], AF.Exp)
                    S_.act(e32[3][:, :], G32[:, :], AF.Exp, bias=mcol[:, 0:1], scale=NEG1)
                    S_.tt(e32[3][:, :], khb[h][:, csl], e32[3][:, :], AO.mult)
                    ptj = ptA if h % 2 == 0 else ptB
                    S_.transpose(ptj[:, 0:128], e32[3][:, :], ident)
                    S_.copy(btm[:, :], ptj[0:128, 0:128], eng="scalar")
                    bst = pbE if h % 2 == 0 else pbF
                    S_.mm(bst[:, 0:128], btm, vtb[h][:, csl], start=True, stop=True)
                    S_.ts(sgla[:, hsl], sgla[:, hsl],
                          egl[:, h * NCH + c:h * NCH + c + 1], None, AO.mult)
                    S_.tt(sgla[:, hsl], sgla[:, hsl], bst[0:128, 0:128], AO.add)

            for h in range(H):
                S_.dma("scalar", kh_d[h][:, hsl_t], khb[h][:, :])
                S_.dma("scalar", gk_d[h][:, hsl_t], gkb[h][:, :])
                S_.dma("scalar", vt_d[h][:, hsl_t], vtb[h][:, :])
            b1.close()
            S_.free(*[f"p1h{hh}_khb{h}" for h in range(H)],
                    *[f"p1h{hh}_gkb{h}" for h in range(H)],
                    *[f"p1h{hh}_vtb{h}" for h in range(H)],
                    f"p1h{hh}_bstr0", f"p1h{hh}_bstr1", f"p1h{hh}_kg1",
                    f"p1h{hh}_wkg1s", f"p1h{hh}_wkg2s", f"p1h{hh}_sg2v")

        S_.copy(sglab[:, :], sgla[:, :])
        S_.dma("sync", agb_in, sglab[0:64, :])
        S_.collective("AllGather", AO.bypass, rg, [agb_in], [agb_out])
        ba = ExitStack()
        agrb = [ba.enter_context(nc.sbuf_tensor(f"agrb{i}", [64, 2048], BF16))
                for i in range(2)]
        for r in range(NCORE):
            buf = agrb[r % 2]
            S_.dma("sync", buf[:, :], agb_out[r])
            if r == 0:
                S_.ts(sgla[0:64, :], buf[:, :], selw[0:64, 0:1], None, AO.mult)
            else:
                S_.stt(sgla[0:64, :], buf[:, :], selw[0:64, r:r + 1], sgla[0:64, :],
                       AO.mult, AO.add)
        S_.memset(sgla[64:128, :], 0.0)
        S_.copy(sglab[:, :], sgla[:, :])
        ba.close()
        S_.free("agrb0", "agrb1")

        for hh in range(2):
            hsl_t = slice(hh * 512, (hh + 1) * 512)
            b2 = ExitStack()
            def B2(name, shape, dt):
                return b2.enter_context(nc.sbuf_tensor(f"p2h{hh}_" + name, shape, dt))
            qh2 = [B2(f"qh2_{h}", [128, 512], BF16) for h in range(H)]
            kh2 = [B2(f"kh2_{h}", [128, 512], BF16) for h in range(H)]
            gk2 = [B2(f"gk2_{h}", [128, 512], BF16) for h in range(H)]
            vt2 = [B2(f"vt2_{h}", [128, 512], BF16) for h in range(H)]
            bstr2 = [B2(f"bstr2_{i}", [128, 2048], BF16) for i in range(2)]
            scr2 = [B2(f"scr2_{i}", [128, 128], BF16) for i in range(4)]
            gvt = B2("gvt", [128, 512], BF16)
            sq2 = B2("sq2", [128, 512], BF16)

            for h in range(H):
                S_.dma("scalar", kh2[h][:, :], kh_d[h][:, hsl_t])
                S_.dma("scalar", gk2[h][:, :], gk_d[h][:, hsl_t])
                S_.dma("scalar", vt2[h][:, :], vt_d[h][:, hsl_t])

            def q_consume(oc, tt, bank):
                S_.ts(qh2[oc][:, :], bank[:, 0:512], 1.0, None, AO.mult)
            project(wqb_in, 16, q_consume, bstr2, tts=(hh,))

            for cc in range(4):
                c = hh * 4 + cc
                csl = slice(cc * CH, (cc + 1) * CH)
                for h in range(H):
                    hsl = slice(h * 128, (h + 1) * 128)
                    G32, gch, e32 = G32p[h % 2], gchp[h % 2], e32p[h % 2]
                    mcol, btm = mcolp[h % 2], btmp[h % 2]
                    sc0, sc1 = scr2[h % 2 * 2], scr2[h % 2 * 2 + 1]
                    bA = pbC if h % 2 == 0 else pbD
                    S_.ts(gch[:, :], gk2[h][:, csl], 1.0 / 16.0, None, AO.mult)
                    S_.scan(G32[:, :], ones32, gch, 0.0, AO.mult, AO.add)
                    S_.ts(mcol[:, 0:1], G32[:, 127:128], 1.0, None, AO.mult)
                    S_.act(egl[:, h * NCH + c:h * NCH + c + 1], mcol[:, 0:1], AF.Exp)
                    S_.act(e32[3][:, :], G32[:, :], AF.Exp, bias=mcol[:, 0:1], scale=NEG1)
                    S_.ts(mcol[:, 1:2], G32[:, 127:128], 0.5, None, AO.mult)
                    S_.ts(mcol[:, 2:3], mcol[:, 1:2], -1.0, None, AO.mult)
                    S_.act(e32[0][:, :], G32[:, :], AF.Exp, bias=mcol[:, 2:3])
                    S_.act(e32[1][:, :], G32[:, :], AF.Exp, bias=mcol[:, 1:2], scale=NEG1)
                    S_.act(e32[2][:, :], G32[:, :], AF.Exp)
                    S_.tt(sc0[:, :], qh2[h][:, csl], e32[0][:, :], AO.mult)
                    S_.tt(sc1[:, :], kh2[h][:, csl], e32[1][:, :], AO.mult)
                    S_.mm(bA[:, 0:128], sc1, sc0, start=True, stop=True)
                    S_.tt(sc0[:, :], bA[0:128, 0:128], masku, AO.mult)
                    S_.tt(sc1[:, :], qh2[h][:, csl], e32[2][:, :], AO.mult)
                    S_.mm(bA[:, 128:256], vt2[h][:, csl], sc0, start=True,
                          stop=False, inc=False)
                    S_.mm(bA[:, 128:256], sglab[:, hsl], sc1, start=False,
                          stop=True, inc=True)
                    S_.copy(qh2[h][:, csl], bA[0:128, 128:256])
                    S_.tt(e32[3][:, :], kh2[h][:, csl], e32[3][:, :], AO.mult)
                    ptj = ptA if h % 2 == 0 else ptB
                    S_.transpose(ptj[:, 0:128], e32[3][:, :], ident)
                    S_.copy(btm[:, :], ptj[0:128, 0:128], eng="scalar")
                    bK = pbE if h % 2 == 0 else pbF
                    S_.mm(bK[:, 0:128], btm, vt2[h][:, csl], start=True, stop=True)
                    S_.ts(sgla[:, hsl], sgla[:, hsl],
                          egl[:, h * NCH + c:h * NCH + c + 1], None, AO.mult)
                    S_.tt(sgla[:, hsl], sgla[:, hsl], bK[0:128, 0:128], AO.add)
                    S_.copy(sglab[:, hsl], sgla[:, hsl])

            for h in range(H):
                S_.act(sq2[:, :], qh2[h][:, :], AF.Square)
                S_.mm(pbC[0:1, 0:512], onecol, qh2[h][:, :], start=True, stop=True)
                S_.mm(pbD[0:1, 0:512], onecol, sq2[:, :], start=True, stop=True)
                mu = rows[0:1, 0:512]
                musq = rows[0:1, 512:1024]
                var = rows[0:1, 1024:1536]
                rstd = rows[0:1, 1536:2048]
                S_.ts(mu, pbC[0:1, 0:512], 1.0 / 128.0, None, AO.mult)
                S_.tt(musq, mu, mu, AO.mult)
                S_.stt(var, pbD[0:1, 0:512], 1.0 / 128.0, musq, AO.mult, AO.subtract)
                S_.act(var, var, AF.Sqrt, bias=ccol[0:1, 2:3])
                S_.recip(rstd, var)
                S_.copy(rowb[0:1, 0:512], rstd)
                S_.stt(musq, mu, -1.0, rstd, AO.mult, AO.mult)
                S_.copy(rowb[0:1, 512:1024], musq)
                S_.mm(pbC[:, 0:512], onerow, rowb[0:1, 0:512], start=True, stop=True)
                S_.mm(pbD[:, 0:512], onerow, rowb[0:1, 512:1024], start=True, stop=True)
                S_.ts(sq2[:, :], pbC[0:128, 0:512], 1.0, None, AO.mult)
                S_.tt(qh2[h][:, :], qh2[h][:, :], sq2[:, :], AO.mult)
                S_.ts(sq2[:, :], pbD[0:128, 0:512], 1.0, None, AO.mult)
                S_.tt(qh2[h][:, :], qh2[h][:, :], sq2[:, :], AO.add)

            def g_consume(oc, tt, bank):
                S_.act(gvt[:, :], bank[:, 0:512], AF.Silu, bias=bgc[:, oc:oc + 1])
                S_.tt(qh2[oc][:, :], qh2[oc][:, :], gvt[:, :], AO.mult)
            project(wgb_in, 16, g_consume, bstr2, tts=(hh,))

            def wob_consume(oc, tt, bank):
                S_.stt(xt[oc][:, hsl_t], bank[:, 0:512], 1.0, xt[oc][:, hsl_t],
                       AO.mult, AO.add)
            project(wob_in, 16, wob_consume, bstr2, rhs=qh2, tts=(0,))

            b2.close()
            S_.free(*[f"p2h{hh}_qh2_{h}" for h in range(H)],
                    *[f"p2h{hh}_kh2_{h}" for h in range(H)],
                    *[f"p2h{hh}_gk2_{h}" for h in range(H)],
                    *[f"p2h{hh}_vt2_{h}" for h in range(H)],
                    f"p2h{hh}_bstr2_0", f"p2h{hh}_bstr2_1", f"p2h{hh}_scr2_0",
                    f"p2h{hh}_scr2_1", f"p2h{hh}_gvt", f"p2h{hh}_sq2")

        b3.close()
        S_.free("sgla", "sglab", "egl",
                *[f"G32_{i}" for i in range(2)], *[f"gch_{i}" for i in range(2)],
                *[f"e32{i}_{j}" for i in range(4) for j in range(2)],
                *[f"mcol_{i}" for i in range(2)], *[f"btm_{i}" for i in range(2)])

    if debug:
        for i in range(NDC):
            S_.dma("sync", dbg["d_xb"][i * 128:(i + 1) * 128, :], xt[i][:, :])

    if upto >= 4:
        ffn(False)
    for i in range(NDC):
        S_.dma("sync", out_d[i * 128:(i + 1) * 128, :], xt[i][:, :])

    final = dict(S_.cnt)
    S_.prog["sync"].append(lambda e, v=final["lds"], h=S_.semh["lds"]: e.wait_ge(h, v))
    S_.prog["scalar"].append(lambda e, v=final["lda"], h=S_.semh["lda"]: e.wait_ge(h, v))
    S_.prog["gpsimd"].append(lambda e, v=final["ldg"], h=S_.semh["ldg"]: e.wait_ge(h, v))

    with nc.Block() as block:
        @block.sync
        def _(eng):
            for fn in S_.prog["sync"]:
                fn(eng)

        @block.scalar
        def _(eng):
            for fn in S_.prog["scalar"]:
                fn(eng)

        @block.tensor
        def _(eng):
            for fn in S_.prog["tensor"]:
                fn(eng)

        @block.vector
        def _(eng):
            for fn in S_.prog["vector"]:
                fn(eng)

        @block.gpsimd
        def _(eng):
            for fn in S_.prog["gpsimd"]:
                fn(eng)

    big.close()
    return nc


def _strips(w):
    O, K = w.shape
    onc = (O + 127) // 128
    if O % 128:
        w = np.pad(w, ((0, onc * 128 - O), (0, 0)))
    a = w.reshape(onc, 128, K // 128, 128)
    return np.ascontiguousarray(a.transpose(0, 3, 2, 1)).reshape(onc, 128, K)


def _pad_heads(w, H=16, dh=64):
    K = w.shape[1]
    out = np.zeros((H * 128, K), w.dtype)
    out.reshape(H, 128, K)[:, :dh] = w.reshape(H, dh, K)
    return out


def make_in_maps(L0_attn_a_x, L0_attn_a_Wq, L0_attn_a_Wk, L0_attn_a_Wv,
                 L0_attn_a_Wo, L0_attn_b_Wq, L0_attn_b_Wk, L0_attn_b_Wkg1,
                 L0_attn_b_Wkg2, L0_attn_b_bkg2, L0_attn_b_Wv, L0_attn_b_Wg,
                 L0_attn_b_bg, L0_attn_b_Wo, L0_ffn_w_gate, L0_ffn_w_up,
                 L0_ffn_w_down):
    f32 = np.float32
    shared = {}
    shared["wqa"] = _strips(np.asarray(L0_attn_a_Wq, f32))
    shared["wka"] = _strips(np.asarray(L0_attn_a_Wk, f32))
    shared["wva"] = _strips(np.asarray(L0_attn_a_Wv, f32))
    shared["woa"] = _strips(np.asarray(L0_attn_a_Wo, f32))
    shared["wqb"] = _strips(_pad_heads(np.asarray(L0_attn_b_Wq, f32)))
    shared["wkb"] = _strips(_pad_heads(np.asarray(L0_attn_b_Wk, f32)))
    shared["wvb"] = _strips(np.asarray(L0_attn_b_Wv, f32))
    shared["wgb"] = _strips(np.asarray(L0_attn_b_Wg, f32))
    shared["wob"] = _strips(np.asarray(L0_attn_b_Wo, f32))
    wkg1 = np.asarray(L0_attn_b_Wkg1, f32)
    shared["wkg1t"] = np.ascontiguousarray(
        wkg1.T.reshape(16, 128, 16).transpose(1, 0, 2).reshape(128, 256))
    wkg2 = np.asarray(L0_attn_b_Wkg2, f32)
    wkg2p = np.zeros((16, 2048), f32)
    wkg2p.reshape(16, 16, 128)[:, :, :64] = wkg2.reshape(16, 64, 16).transpose(2, 0, 1)
    shared["wkg2p"] = wkg2p
    bkg2p = np.zeros((128, 16), f32)
    bkg2p[:64, :] = np.asarray(L0_attn_b_bkg2, f32).reshape(16, 64).T
    shared["bkg2p"] = bkg2p
    shared["bgc"] = np.ascontiguousarray(
        np.asarray(L0_attn_b_bg, f32).reshape(16, 128).T)
    wg = np.asarray(L0_ffn_w_gate, f32)
    wu = np.asarray(L0_ffn_w_up, f32)
    wd = np.asarray(L0_ffn_w_down, f32)
    shared["wgf"] = _strips(wg)
    shared["wuf"] = _strips(wu)
    wdp = np.pad(wd, ((0, 0), (0, I_PAD - I_RAW)))
    aD = wdp.reshape(16, 128, NIC, 128)
    shared["wdf"] = np.ascontiguousarray(aD.transpose(0, 3, 2, 1)).reshape(
        16, 128, NIC * 128)
    shared["ident"] = np.eye(128, dtype=f32).astype(BF)
    shared["masku"] = np.triu(np.ones((128, 128), f32)).astype(BF)
    protm = np.zeros((128, 128), f32)
    for m in range(64):
        protm[m + 64, m] = -1.0
    for m in range(64, 128):
        protm[m - 64, m] = 1.0
    shared["prot"] = protm.astype(BF)
    shared["onecol"] = np.ones((128, 1), f32).astype(BF)
    shared["onerow"] = np.ones((1, 128), f32).astype(BF)

    HD = 128
    inv = 1.0 / 10000.0 ** (np.arange(0, HD, 2, dtype=np.float64) / HD)
    tpos = np.arange(S, dtype=np.float64)
    fr = np.outer(tpos, inv)
    emb = np.concatenate([fr, fr], -1)
    cosf = np.cos(emb).astype(f32)
    sinf = np.sin(emb).astype(f32)

    wsl3 = np.concatenate([wg.reshape(8, -1), wu.reshape(8, -1),
                           wd.reshape(8, -1)], axis=1)

    x = np.asarray(L0_attn_a_x, f32)
    in_maps = []
    for c in range(NCORE):
        b, hf = c // 2, c % 2
        m = dict(shared)
        m["x"] = np.ascontiguousarray(x[b, hf * T:(hf + 1) * T, :].T)
        m["cosr"] = np.ascontiguousarray(cosf[hf * T:(hf + 1) * T, :].T)
        m["sinr"] = np.ascontiguousarray(sinf[hf * T:(hf + 1) * T, :].T)
        sel = np.zeros((128, 8), f32)
        if hf == 1:
            sel[:, c - 1] = 1.0
        m["selw"] = sel
        m["wsl"] = np.ascontiguousarray(wsl3[c].reshape(128, 3 * 12928))
        in_maps.append(m)
    return in_maps


def kernel(**inputs):
    in_maps = make_in_maps(**inputs)
    nc = build_program(False)
    res = run_bass_kernel_spmd(nc, in_maps, core_ids=list(range(NCORE)))
    out = np.empty((B, S, D), np.float32)
    for c in range(NCORE):
        b, hf = c // 2, c % 2
        out[b, hf * T:(hf + 1) * T, :] = np.asarray(res.results[c]["out"]).T
    return out
